# version 02
# baseline (speedup 1.0000x reference)
import contextlib
import numpy as np
import ml_dtypes
import concourse.bass as bass
import concourse.mybir as mybir
from concourse.bass_utils import run_bass_kernel_spmd

F32 = mybir.dt.float32
BF16 = mybir.dt.bfloat16
I32 = mybir.dt.int32
AF = mybir.ActivationFunctionType
ALU = mybir.AluOpType

D = 4096
S = 4096
RO = 2048
HT = 32
QH = 32
KVH = 8
INTER = 11008
I2 = 22016
PC = 2752
TG = 172
TD = 86
SBOUND = [1024, 2048, 3072, 4096]
NST = [8, 16, 24, 32]
CHUNKS = [[0, 3, 4, 7], [1, 2, 5, 6]]
EPS = 1e-5
SCALE = float(1.0 / np.sqrt(128.0))
NEG = -60000.0

ENGS = ("sync", "scalar", "vector", "tensor", "gpsimd")


def _perm_block(n):
    idx = np.arange(n)
    t, rem = idx // 128, idx % 128
    j, v = rem // 16, rem % 16
    return 128 * t + 8 * v + j


PH_PERM = _perm_block(D)
PI_PERM = _perm_block(I2)
PI_DN = _perm_block(INTER)


class Sem:
    def __init__(self, nc, es, name):
        self.h = es.enter_context(nc.semaphore(name))
        self.n = 0

    def add(self, k):
        self.n += k
        return self.n


class Plan:
    def __init__(self):
        self.ops = {e: [] for e in ENGS}

    def emit(self, eng, fn):
        self.ops[eng].append(fn)


def build_program(dbg=False, upto=99):
    nc = bass.Bass("TRN2", target_bir_lowering=False)
    es = contextlib.ExitStack()
    plan = Plan()
    skind = "ExternalOutput" if dbg else "Internal"

    def din(name, shape, dt):
        return nc.dram_tensor(name, shape, dt, kind="ExternalInput").ap()

    x_kv = din("x_kv", [S, D], F32)
    x_own = din("x_own", [RO, D], F32)
    wqT = din("wqT", [D, D], F32)
    wkT = din("wkT", [D, 1024], F32)
    wvT = din("wvT", [D, 1024], F32)
    woT = din("woT", [D, D], F32)
    cos_k = din("cos_k", [128, S], F32)
    sin_k = din("sin_k", [128, S], F32)
    cos_q = din("cos_q", [128, RO], F32)
    sin_q = din("sin_q", [128, RO], F32)
    qw_gu = din("qw_gu", [D, PC], I32)
    qz_gu = din("qz_gu", [32, PC], I32)
    sc_gu = din("sc_gu", [32, I2], F32)
    qw_dn = din("qw_dn", [INTER, 512], I32)
    qz_dn = din("qz_dn", [TD, 512], I32)
    sc_dn = din("sc_dn", [TD, D], F32)
    dmask = din("dmask", [128, 4, 8, 512], BF16)
    biascfg = din("biascfg", [128, 4, 32], F32)

    out = nc.dram_tensor("out", [RO, D], F32, kind="ExternalOutput").ap()

    def scr(name, shape, dt):
        return nc.dram_tensor(name, shape, dt, kind=skind).ap()

    zrow_gu = scr("zrow_gu", [32, I2], BF16)
    srow_gu = scr("srow_gu", [32, I2], BF16)
    zrow_dn = scr("zrow_dn", [TD, D], BF16)
    srow_dn = scr("srow_dn", [TD, D], BF16)
    Wgu_h = scr("Wgu_h", [TG, D, 128], BF16)
    Wdn_h = scr("Wdn_h", [TD, 128, D], BF16)
    LNxT_h = scr("LNxT_h", [HT, 128, S], BF16)
    LNoT_h = scr("LNoT_h", [HT, 128, RO], BF16)
    KT_h = scr("KT_h", [KVH, 128, S], BF16)
    V_h = scr("V_h", [KVH, 32, 128, 128], BF16)
    QT_h = scr("QT_h", [QH, 128, RO], BF16)
    OT_h = scr("OT_h", [QH, 128, RO], BF16)
    YT_h = scr("YT_h", [HT, 128, RO], BF16)
    y1_h = scr("y1_h", [RO, D], F32)
    lnyT_h = scr("lnyT_h", [HT, 128, RO], BF16)
    H_h = scr("H_h", [TD, 128, RO], BF16)
    den_h = scr("den_h", [4, QH, 512], F32)

    ident_t = nc.inline_tensor(np.eye(128, dtype=ml_dtypes.bfloat16), name="identb")
    ones_t = nc.inline_tensor(np.ones((128, 1), dtype=ml_dtypes.bfloat16), name="onesb")

    LD = Sem(nc, es, "LD")
    ST = Sem(nc, es, "ST")
    GP = Sem(nc, es, "GP")
    TE = Sem(nc, es, "TE")
    VE = Sem(nc, es, "VE")
    SE = Sem(nc, es, "SE")
    PB = Sem(nc, es, "PB")

    phase_no = [0]

    def barrier():
        target = 5 * (phase_no[0] + 1)
        for e in ENGS:
            plan.emit(e, lambda eng: eng.sem_inc(PB.h, 1))
        for e in ENGS:
            plan.emit(e, lambda eng, t=target: eng.wait_ge(PB.h, t))
        phase_no[0] += 1
        PB.n = target

    def dma(queue, o, i, sem):
        def fn(eng, o=o, i=i, s=sem.h):
            eng.dma_start(out=o, in_=i).then_inc(s, 16)
        plan.emit(queue, fn)
        return sem.add(16)

    def wait(eng, sem, tag):
        if tag and tag > 0:
            plan.emit(eng, lambda e, s=sem.h, t=tag: e.wait_ge(s, t))

    def op(eng, fn, sem=None):
        if sem is None:
            plan.emit(eng, fn)
            return None
        tag = sem.add(1)

        def wrapped(e, fn=fn, s=sem.h):
            fn(e).then_inc(s, 1)
        plan.emit(eng, wrapped)
        return tag

    def gsb(name, shape, dt):
        return es.enter_context(nc.sbuf_tensor(name, shape, dt))

    ident = gsb("g_ident", [128, 128], BF16)
    ones = gsb("g_ones", [128, 1], BF16)
    bias_sb = gsb("g_bias", [128, 4, 32], F32)
    eps_sb = gsb("g_eps", [128, 1], F32)
    plan.emit("vector", lambda e: e.memset(eps_sb[:], EPS))
    dma("sync", ident[:], ident_t.ap(), LD)
    dma("sync", ones[:], ones_t.ap(), LD)
    gtag = dma("sync", bias_sb[:], biascfg[:], LD)
    for e in ("vector", "tensor", "scalar", "gpsimd"):
        wait(e, LD, gtag)

    def vfence(tag):
        wait("vector", VE, tag)

    def emit_ln_tile(xb, stat, lnx, sq_scr, tags, key, prev_key):
        t1 = op("vector", lambda e: e.reduce_sum(stat[:, 0:1], xb[:],
                                                 axis=mybir.AxisListType.X), VE)
        wait("scalar", VE, tags.get(("vvfree", prev_key), 0))
        stg = op("scalar", lambda e: e.activation(sq_scr[:], xb[:], AF.Square,
                                                  accum_out=stat[:, 1:2]), SE)
        wait("vector", SE, stg)
        vfence(t1)
        t2 = op("vector", lambda e: e.tensor_scalar(stat[:, 2:3], stat[:, 0:1], 1.0 / D,
                                                    None, ALU.mult), VE)
        t3 = op("vector", lambda e: e.tensor_scalar(stat[:, 3:4], stat[:, 1:2], 1.0 / D,
                                                    None, ALU.mult), VE)
        vfence(t2)
        t4 = op("vector", lambda e: e.tensor_tensor(stat[:, 4:5], stat[:, 2:3], stat[:, 2:3],
                                                    ALU.mult), VE)
        vfence(t4)
        vv = op("vector", lambda e: e.tensor_tensor(stat[:, 5:6], stat[:, 3:4], stat[:, 4:5],
                                                    ALU.subtract), VE)
        tags[("vvfree", key)] = vv
        wait("scalar", VE, vv)
        sst = op("scalar", lambda e: e.activation(stat[:, 6:7], stat[:, 5:6], AF.Sqrt,
                                                  bias=eps_sb[:]), SE)
        wait("vector", SE, sst)
        t5 = op("vector", lambda e: e.reciprocal(stat[:, 7:8], stat[:, 6:7]), VE)
        vfence(t5)
        vl = op("vector", lambda e: e.tensor_scalar(lnx[:], xb[:], stat[:, 2:3], stat[:, 7:8],
                                                    ALU.subtract, ALU.mult), VE)
        tags[("lnx", key)] = vl
        return vl

    def emit_transpose_tile(lnx, dst, rcol, ps_tr, tags, gidx_base, vl_tag):
        wait("tensor", VE, vl_tag)
        last = 0
        for g in range(8):
            bk = (gidx_base + g) % 2
            wait("tensor", SE, tags.get(("trev", gidx_base + g - 2), 0))

            def trf(e, g=g, bk=bk, lnx=lnx):
                lastmm = None
                for u in range(4):
                    ht = g * 4 + u
                    lastmm = e.matmul(ps_tr[bk][:, u * 128:(u + 1) * 128],
                                      lnx[:, ht * 128:(ht + 1) * 128], ident[:],
                                      start=(u == 0), stop=(u == 3), is_transpose=True)
                return lastmm
            tt = op("tensor", trf, TE)
            wait("scalar", TE, tt)
            sev = op("scalar", lambda e, g=g, bk=bk, dst=dst, rcol=rcol: e.copy(
                dst[:, g * 4:(g + 1) * 4, rcol:rcol + 128],
                ps_tr[bk][:].rearrange("p (t r) -> p t r", r=128)), SE)
            tags[("trev", gidx_base + g)] = sev
            last = sev
        return last

    def emit_rope(psrc, cosb, sinb, coloff, rbufs, dst, tags=None):
        rot, t1, t2 = rbufs
        op("vector", lambda e: e.tensor_copy(rot[0:64, :], psrc[64:128, :]))
        op("vector", lambda e: e.tensor_copy(rot[64:128, :], psrc[0:64, :]))
        op("vector", lambda e: e.tensor_tensor(t1[:], psrc[:], cosb[:, coloff:coloff + 512],
                                               ALU.mult))
        op("vector", lambda e: e.tensor_tensor(t2[:], rot[:], sinb[:, coloff:coloff + 512],
                                               ALU.mult))
        return op("vector", lambda e: e.tensor_tensor(dst[:], t1[:], t2[:], ALU.add), VE)

    def _phase_w0():
      with contextlib.ExitStack() as ph:
          _sb = lambda n, s, d: ph.enter_context(nc.sbuf_tensor(n, s, d))
          with contextlib.ExitStack() as ph2:
              _sb2 = lambda n, s, d: ph2.enter_context(nc.sbuf_tensor(n, s, d))
              zq = _sb2("w0_zq", [32, PC], I32)
              zsm = _sb2("w0_zsm", [32, PC], I32)
              zb = _sb2("w0_zb", [32, I2], BF16)
              scf = _sb2("w0_scf", [32, INTER], F32)
              scb = _sb2("w0_scb", [32, I2], BF16)
              t1 = dma("sync", zq[:], qz_gu[:], LD)
              t2a = dma("sync", scf[:], sc_gu[:, 0:INTER], LD)
              wait("vector", LD, t2a)
              for j in range(8):
                  op("vector", lambda e, j=j: e.tensor_scalar(
                      zsm[:], zq[:], 4 * j, 15, ALU.logical_shift_right, ALU.bitwise_and))
                  op("vector", lambda e, j=j: e.tensor_copy(
                      zb[:].rearrange("g (t b v) -> g t b v", t=TG, b=8, v=16)[:, :, j, :],
                      zsm[:].rearrange("g (t v) -> g t v", v=16)))
              v1 = op("vector", lambda e: e.tensor_copy(scb[:, 0:INTER], scf[:]), VE)
              wait("sync", VE, v1)
              t2b = dma("sync", scf[:], sc_gu[:, INTER:I2], LD)
              wait("vector", LD, t2b)
              v2 = op("vector", lambda e: e.tensor_copy(scb[:, INTER:I2], scf[:]), VE)
              wait("scalar", VE, v2)
              dma("scalar", zrow_gu[:], zb[:], ST)
              stg = dma("scalar", srow_gu[:], scb[:], ST)
              wait("sync", ST, stg)
          zq2 = _sb("w0_zq2", [TD, 512], I32)
          zsm2 = _sb("w0_zsm2", [TD, 512], I32)
          zb2 = _sb("w0_zb2", [TD, D], BF16)
          scf2 = _sb("w0_scf2", [TD, D], F32)
          scb2 = _sb("w0_scb2", [TD, D], BF16)
          t3 = dma("sync", zq2[:], qz_dn[:], LD)
          t4 = dma("sync", scf2[:], sc_dn[:], LD)
          wait("vector", LD, t4)
          for j in range(8):
              op("vector", lambda e, j=j: e.tensor_scalar(
                  zsm2[:], zq2[:], 4 * j, 15, ALU.logical_shift_right, ALU.bitwise_and))
              op("vector", lambda e, j=j: e.tensor_copy(
                  zb2[:].rearrange("g (T b v) -> g T b v", T=HT, b=8, v=16)[:, :, j, :],
                  zsm2[:].rearrange("g (T v) -> g T v", v=16)))
          vt = op("vector", lambda e: e.tensor_copy(scb2[:], scf2[:]), VE)
          wait("scalar", VE, vt)
          dma("scalar", zrow_dn[:], zb2[:], ST)
          stg = dma("scalar", srow_dn[:], scb2[:], ST)
          wait("sync", ST, stg)
          barrier()
    _phase_w0()
    if upto < 1:
        return finish(nc, plan, ST)

    def _phase_w1():
      with contextlib.ExitStack() as ph:
          _sb = lambda n, s, d: ph.enter_context(nc.sbuf_tensor(n, s, d))
          NQ = 4
          QP = PC // NQ
          QI = QP * 8
          TQ = TG // NQ
          qwb = [_sb(f"w1_qw{i}", [128, QP], I32) for i in range(2)]
          wint = _sb("w1_wint", [128, QI], I32)
          wbf = _sb("w1_wbf", [128, QI], BF16)
          wd = [_sb(f"w1_wd{i}", [128, QI], BF16) for i in range(2)]
          zbc = [_sb(f"w1_zb{i}", [128, QI], BF16) for i in range(2)]
          sbc = [_sb(f"w1_sb{i}", [128, QI], BF16) for i in range(2)]
          tags = {}
          it = 0

          def w1_quant(ltag, g2, p, n_i, store_fn, it):
              wait("vector", LD, ltag)
              wait("vector", GP, g2)
              wait("vector", ST, tags.get(("st", it - 2), 0))
              nb = n_i // 128
              for j in range(8):
                  op("vector", lambda e, j=j, p=p, nb=nb: e.tensor_scalar(
                      wint[:, 0:nb * 128].rearrange("p (t b v) -> p t b v", t=nb, b=8, v=16)[:, :, j, :],
                      qwb[p][:, 0:nb * 16].rearrange("p (t v) -> p t v", v=16),
                      4 * j, 15, ALU.logical_shift_right, ALU.bitwise_and))
              op("vector", lambda e, n=n_i: e.tensor_copy(wbf[:, 0:n], wint[:, 0:n]))
              op("vector", lambda e, p=p, n=n_i: e.tensor_tensor(
                  wd[p][:, 0:n], wbf[:, 0:n], zbc[p][:, 0:n], ALU.subtract))
              vt = op("vector", lambda e, p=p, n=n_i: e.tensor_tensor(
                  wd[p][:, 0:n], wd[p][:, 0:n], sbc[p][:, 0:n], ALU.mult), VE)
              tags[("vfree", it)] = vt
              wait("scalar", VE, vt)
              tags[("st", it)] = store_fn()

          for ht in range(HT):
              for q in range(NQ):
                  p = it % 2
                  wait("sync", VE, tags.get(("vfree", it - 2), 0))
                  ltag = dma("sync", qwb[p][:],
                             qw_gu[ht * 128:(ht + 1) * 128, q * QP:(q + 1) * QP], LD)
                  wait("gpsimd", VE, tags.get(("vfree", it - 2), 0))
                  dma("gpsimd", zbc[p][:],
                      zrow_gu[ht:ht + 1, q * QI:(q + 1) * QI].partition_broadcast(128), GP)
                  g2 = dma("gpsimd", sbc[p][:],
                           srow_gu[ht:ht + 1, q * QI:(q + 1) * QI].partition_broadcast(128), GP)

                  def store(p=p, ht=ht, q=q):
                      return dma("scalar",
                                 Wgu_h[q * TQ:(q + 1) * TQ, ht * 128:(ht + 1) * 128, :]
                                 .rearrange("t p i -> p t i"),
                                 wd[p][:].rearrange("p (t i) -> p t i", i=128), ST)
                  w1_quant(ltag, g2, p, QI, store, it)
                  it += 1
          for t in range(TD):
              p = it % 2
              wait("sync", VE, tags.get(("vfree", it - 2), 0))
              ltag = dma("sync", qwb[p][:, 0:512], qw_dn[t * 128:(t + 1) * 128, :], LD)
              wait("gpsimd", VE, tags.get(("vfree", it - 2), 0))
              dma("gpsimd", zbc[p][:, 0:D], zrow_dn[t:t + 1, :].partition_broadcast(128), GP)
              g2 = dma("gpsimd", sbc[p][:, 0:D], srow_dn[t:t + 1, :].partition_broadcast(128), GP)

              def store(p=p, t=t):
                  return dma("scalar", Wdn_h[t, :, :], wd[p][:, 0:D], ST)
              w1_quant(ltag, g2, p, D, store, it)
              it += 1
          wait("sync", ST, tags[("st", it - 1)])
          barrier()
    _phase_w1()
    if upto < 2:
        return finish(nc, plan, ST)

    def emit_kv_phase(which):
        with contextlib.ExitStack() as ph:
            _sb = lambda n, s, d: ph.enter_context(nc.sbuf_tensor(n, s, d))
            _ps = lambda n, s, d=F32: ph.enter_context(nc.psum_tensor(n, s, d))
            w_sb = _sb(f"{which}_w", [128, HT, 1024], BF16)
            gw = dma("gpsimd", w_sb[:],
                     (wkT if which == "k" else wvT)[:].rearrange("(t p) d -> p t d", p=128), GP)
            ps_mm = [_ps(f"{which}_pm{i}", [128, 512]) for i in range(2)]
            tags = {}
            if which == "k":
                lnxT = _sb("k_lnxT", [128, HT, 512], BF16)
                xb = [_sb(f"k_x{i}", [128, D], F32) for i in range(2)]
                sq_scr = _sb("k_sq", [128, D], F32)
                stat = [_sb(f"k_st{i}", [128, 8], F32) for i in range(2)]
                lnx = [_sb(f"k_lnx{i}", [128, D], BF16) for i in range(2)]
                ps_tr = [_ps(f"k_ptr{i}", [128, 512], BF16) for i in range(2)]
                cosb = [_sb(f"k_cos{i}", [128, 512], F32) for i in range(2)]
                sinb = [_sb(f"k_sin{i}", [128, 512], F32) for i in range(2)]
                rope_b = [_sb(f"k_r{u}", [128, 512], F32) for u in range(3)]
                kt_sb = [_sb(f"k_kt{i}", [128, 512], BF16) for i in range(2)]
            else:
                lnxTb = [_sb(f"v_lnxT{i}", [128, HT, 512], BF16) for i in range(2)]
                v_sb = [_sb(f"v_o{i}", [128, 512], BF16) for i in range(2)]
            wait("tensor", GP, gw)
            for sc in range(8):
                if which == "k":
                    pcs = sc % 2
                    wait("sync", VE, tags.get(("rope", (sc - 2) * 8 + 7), 0))
                    dma("sync", cosb[pcs][:], cos_k[:, sc * 512:(sc + 1) * 512], LD)
                    cstag = dma("sync", sinb[pcs][:], sin_k[:, sc * 512:(sc + 1) * 512], LD)
                    tags[("cosld", sc)] = cstag
                    for rt in range(4):
                        i = sc * 4 + rt
                        pi = i % 2
                        wait("sync", VE, tags.get(("lnx", i - 2), 0))
                        lx = dma("sync", xb[pi][:], x_kv[i * 128:(i + 1) * 128, :], LD)
                        wait("vector", LD, lx)
                        wait("scalar", LD, lx)
                        vl = emit_ln_tile(xb[pi], stat[pi], lnx[pi], sq_scr, tags, i, i - 2)
                        if sc > 0 and rt == 0:
                            wait("tensor", ST, tags[("lnspill", sc - 1)])
                            wait("scalar", TE, tags[("kmms", sc - 1)])
                        emit_transpose_tile(lnx[pi], lnxT, rt * 128, ps_tr, tags, i * 8, vl)
                    lastev = tags[("trev", (sc * 4 + 3) * 8 + 7)]
                    wait("scalar", SE, lastev)
                    tags[("lnspill", sc)] = dma(
                        "scalar", LNxT_h[:, :, sc * 512:(sc + 1) * 512].rearrange("t p s -> p t s"),
                        lnxT[:], ST)
                    wait("tensor", SE, lastev)
                    for dt in range(8):
                        kg = sc * 8 + dt
                        bk = kg % 2
                        wait("tensor", VE, tags.get(("rope", kg - 2), 0))

                        def kmm(e, dt=dt, bk=bk):
                            last = None
                            for ht in range(HT):
                                last = e.matmul(ps_mm[bk][:], w_sb[:, ht, dt * 128:(dt + 1) * 128],
                                                lnxT[:, ht, :], start=(ht == 0), stop=(ht == HT - 1))
                            return last
                        tmm = op("tensor", kmm, TE)
                        wait("vector", TE, tmm)
                        wait("vector", ST, tags.get(("ktst", kg - 2), 0))
                        if dt == 0:
                            wait("vector", LD, tags[("cosld", sc)])
                        rtag = emit_rope(ps_mm[bk], cosb[pcs], sinb[pcs], 0, rope_b, kt_sb[bk])
                        tags[("rope", kg)] = rtag
                        wait("scalar", VE, rtag)
                        tags[("ktst", kg)] = dma(
                            "scalar", KT_h[dt, :, sc * 512:(sc + 1) * 512], kt_sb[bk][:], ST)
                    tags[("kmms", sc)] = TE.n
                else:
                    pc_ = sc % 2
                    wait("sync", TE, tags.get(("vmms", sc - 2), 0))
                    lt = dma("sync", lnxTb[pc_][:],
                             LNxT_h[:, :, sc * 512:(sc + 1) * 512].rearrange("t p s -> p t s"), LD)
                    wait("tensor", LD, lt)
                    for sl in range(4):
                        for dc in range(2):
                            g = sc * 8 + sl * 2 + dc
                            bk = g % 2
                            wait("tensor", SE, tags.get(("vevac", g - 2), 0))

                            def vmm(e, sl=sl, dc=dc, bk=bk, pc_=pc_):
                                last = None
                                for ht in range(HT):
                                    last = e.matmul(ps_mm[bk][:],
                                                    lnxTb[pc_][:, ht, sl * 128:(sl + 1) * 128],
                                                    w_sb[:, ht, dc * 512:(dc + 1) * 512],
                                                    start=(ht == 0), stop=(ht == HT - 1))
                                return last
                            tmm = op("tensor", vmm, TE)
                            wait("scalar", TE, tmm)
                            wait("scalar", ST, tags.get(("vst", g - 2), 0))
                            sev = op("scalar", lambda e, bk=bk: e.copy(v_sb[bk][:], ps_mm[bk][:]), SE)
                            tags[("vevac", g)] = sev
                            tags[("vst", g)] = dma(
                                "scalar",
                                V_h[dc * 4:(dc + 1) * 4, sc * 4 + sl, :, :].rearrange("k p d -> p k d"),
                                v_sb[bk][:].rearrange("p (k d) -> p k d", d=128), ST)
                    tags[("vmms", sc)] = TE.n
            last_st = max(v for k, v in tags.items() if k[0] in ("ktst", "vst", "lnspill"))
            wait("sync", ST, last_st)
        barrier()

    emit_kv_phase("k")
    if upto < 3:
        return finish(nc, plan, ST)
    emit_kv_phase("v")
    if upto < 4:
        return finish(nc, plan, ST)

    def _phase_q1():
      with contextlib.ExitStack() as ph:
          _sb = lambda n, s, d: ph.enter_context(nc.sbuf_tensor(n, s, d))
          _ps = lambda n, s, d=F32: ph.enter_context(nc.psum_tensor(n, s, d))
          lnoT = _sb("q1_lnoT", [128, HT, 512], BF16)
          xb = [_sb(f"q1_x{i}", [128, D], F32) for i in range(2)]
          sq_scr = _sb("q1_sq", [128, D], F32)
          stat = [_sb(f"q1_st{i}", [128, 8], F32) for i in range(2)]
          lnx = [_sb(f"q1_lnx{i}", [128, D], BF16) for i in range(2)]
          ps_tr = [_ps(f"q1_ptr{i}", [128, 512], BF16) for i in range(2)]
          tags = {}
          for sc in range(4):
              for rt in range(4):
                  i = sc * 4 + rt
                  pi = i % 2
                  wait("sync", VE, tags.get(("lnx", i - 2), 0))
                  lx = dma("sync", xb[pi][:], x_own[i * 128:(i + 1) * 128, :], LD)
                  wait("vector", LD, lx)
                  wait("scalar", LD, lx)
                  vl = emit_ln_tile(xb[pi], stat[pi], lnx[pi], sq_scr, tags, i, i - 2)
                  if sc > 0 and rt == 0:
                      wait("tensor", ST, tags[("spill", sc - 1)])
                  emit_transpose_tile(lnx[pi], lnoT, rt * 128, ps_tr, tags, i * 8, vl)
              lastev = tags[("trev", (sc * 4 + 3) * 8 + 7)]
              wait("scalar", SE, lastev)
              tags[("spill", sc)] = dma(
                  "scalar", LNoT_h[:, :, sc * 512:(sc + 1) * 512].rearrange("t p s -> p t s"),
                  lnoT[:], ST)
          wait("sync", ST, tags[("spill", 3)])
          barrier()
    _phase_q1()
    if upto < 5:
        return finish(nc, plan, ST)

    def _phase_q2():
      with contextlib.ExitStack() as ph:
          _sb = lambda n, s, d: ph.enter_context(nc.sbuf_tensor(n, s, d))
          _ps = lambda n, s, d=F32: ph.enter_context(nc.psum_tensor(n, s, d))
          lnq = _sb("q2_lnq", [128, HT, RO], BF16)
          wq_sb = [_sb(f"q2_w{i}", [128, HT, 128], BF16) for i in range(2)]
          cosb = _sb("q2_cos", [128, RO], F32)
          sinb = _sb("q2_sin", [128, RO], F32)
          rope_b = [_sb(f"q2_r{u}", [128, 512], F32) for u in range(3)]
          qt_st = [_sb(f"q2_qt{i}", [128, RO], BF16) for i in range(2)]
          ps_q = [_ps(f"q2_ps{i}", [128, 512]) for i in range(8)]
          dma("sync", cosb[:], cos_q[:], LD)
          dma("sync", sinb[:], sin_q[:], LD)
          lq = dma("sync", lnq[:], LNoT_h[:].rearrange("t p s -> p t s"), LD)
          wait("tensor", LD, lq)
          tags = {}
          for dt in range(QH):
              p = dt % 2
              wait("gpsimd", TE, tags.get(("qmm", dt - 2), 0))
              gw = dma("gpsimd", wq_sb[p][:],
                       wqT[:, dt * 128:(dt + 1) * 128].rearrange("(t p) d -> p t d", p=128), GP)
              wait("tensor", GP, gw)
              wait("tensor", VE, tags.get(("rope", dt - 2, 3), 0))

              def qmm(e, p=p, dt=dt):
                  last = None
                  for ht in range(HT):
                      for rc in range(4):
                          last = e.matmul(ps_q[(dt % 2) * 4 + rc][:], wq_sb[p][:, ht, :],
                                          lnq[:, ht, rc * 512:(rc + 1) * 512],
                                          start=(ht == 0), stop=(ht == HT - 1),
                                          skip_group_check=True)
                  return last
              tmm = op("tensor", qmm, TE)
              tags[("qmm", dt)] = tmm
              wait("vector", TE, tmm)
              wait("vector", ST, tags.get(("qtst", dt - 2), 0))
              for rc in range(4):
                  rtag = emit_rope(ps_q[(dt % 2) * 4 + rc], cosb, sinb, rc * 512, rope_b,
                                   qt_st[p][:, rc * 512:(rc + 1) * 512].rearrange("p r -> p r"))
                  tags[("rope", dt, rc)] = rtag
              wait("scalar", VE, tags[("rope", dt, 3)])
              tags[("qtst", dt)] = dma("scalar", QT_h[dt, :, :], qt_st[p][:], ST)
          wait("sync", ST, tags[("qtst", QH - 1)])
          barrier()
    _phase_q2()
    if upto < 6:
        return finish(nc, plan, ST)

    def _phase_atn():
      with contextlib.ExitStack() as ph:
          _sb = lambda n, s, d: ph.enter_context(nc.sbuf_tensor(n, s, d))
          _ps = lambda n, s, d=F32: ph.enter_context(nc.psum_tensor(n, s, d))
          qt_sb = _sb("a_qt", [128, QH, 512], BF16)
          kt_b = [_sb(f"a_kt{i}", [128, S], BF16) for i in range(2)]
          v_b = [_sb(f"a_v{i}", [128, 32, 128], BF16) for i in range(2)]
          pt = [_sb(f"a_pt{i}", [128, 512], BF16) for i in range(3)]
          dm_sb = _sb("a_dm", [128, 4, 8, 512], BF16)
          ot_st = [_sb(f"a_ot{i}", [128, QH, 512], BF16) for i in range(2)]
          den_sb = [_sb(f"a_dn{i}", [1, 512], F32) for i in range(2)]
          rec_sb = [_sb(f"a_rc{i}", [1, 512], F32) for i in range(2)]
          recb = [_sb(f"a_rb{i}", [128, 512], F32) for i in range(2)]
          ps_s = [_ps(f"a_ps{i}", [128, 512]) for i in range(2)]
          ps_o = [_ps(f"a_po{i}", [128, 512]) for i in range(3)]
          ps_d = [_ps(f"a_pd{i}", [128, 512]) for i in range(2)]
          dma("sync", dm_sb[:], dmask[:], LD)
          tags = {}
          hi = 0
          gt = 0
          for qc in range(4):
              nst = NST[qc]
              wait("sync", TE, tags.get(("qcdone", qc - 1), 0))
              qt_tag = dma("sync", qt_sb[:],
                           QT_h[:, :, qc * 512:(qc + 1) * 512].rearrange("t p r -> p t r"), LD)
              for kvh in range(KVH):
                  ki = qc * KVH + kvh
                  kp = ki % 2
                  wait("sync", TE, tags.get(("kidone", ki - 2), 0))
                  kl = dma("sync", kt_b[kp][:, 0:nst * 128], KT_h[kvh, :, 0:nst * 128], LD)
                  vl_ = dma("sync", v_b[kp][:, 0:nst, :],
                            V_h[kvh, 0:nst, :, :].rearrange("s p d -> p s d"), LD)
                  wait("tensor", LD, vl_)
                  if kvh == 0:
                      wait("tensor", LD, qt_tag)
                  for qh in range(4):
                      head = kvh * 4 + qh
                      bo = hi % 3
                      bd = hi % 2
                      wait("tensor", VE, tags.get(("otn", hi - 3), 0))
                      wait("tensor", SE, tags.get(("dcp", hi - 2), 0))
                      for st in range(nst):
                          bs = gt % 2
                          bp = gt % 3
                          wait("tensor", SE, tags.get(("exp", gt - 2), 0))
                          stag = op("tensor", lambda e, bs=bs, st=st, head=head, kp=kp:
                                    e.matmul(ps_s[bs][:], kt_b[kp][:, st * 128:(st + 1) * 128],
                                             qt_sb[:, head, :], start=True, stop=True,
                                             skip_group_check=True), TE)
                          wait("scalar", TE, stag)
                          wait("scalar", TE, tags.get(("av", gt - 3), 0))
                          etag = op("scalar", lambda e, bs=bs, bp=bp, st=st, qc=qc: e.activation(
                              pt[bp][:], ps_s[bs][:], AF.Exp,
                              bias=bias_sb[:, qc, st:st + 1], scale=SCALE), SE)
                          tags[("exp", gt)] = etag
                          masked = (8 * qc <= st < 8 * qc + 8)
                          if masked:
                              wait("vector", SE, etag)
                              mtag = op("vector", lambda e, bp=bp, st=st, qc=qc: e.tensor_tensor(
                                  pt[bp][:], pt[bp][:], dm_sb[:, qc, st - 8 * qc, :],
                                  ALU.mult), VE)
                              tags[("msk", gt)] = mtag
                          if masked:
                              wait("tensor", VE, tags[("msk", gt)])
                          else:
                              wait("tensor", SE, etag)
                          op("tensor", lambda e, bd=bd, bp=bp, st=st, nst=nst: e.matmul(
                              ps_d[bd][0:1, :], ones[:], pt[bp][:],
                              start=(st == 0), stop=(st == nst - 1), skip_group_check=True))
                          avt = op("tensor", lambda e, bo=bo, bp=bp, st=st, nst=nst, kp=kp: e.matmul(
                              ps_o[bo][:], v_b[kp][:, st, :], pt[bp][:],
                              start=(st == 0), stop=(st == nst - 1), skip_group_check=True), TE)
                          tags[("av", gt)] = avt
                          gt += 1
                      hdone = tags[("av", gt - 1)]
                      wait("scalar", TE, hdone)
                      wait("scalar", VE, tags.get(("rec", hi - 2), 0))
                      dtag = op("scalar", lambda e, bd=bd, hp=hi % 2: e.copy(
                          den_sb[hp][:], ps_d[bd][0:1, :]), SE)
                      tags[("dcp", hi)] = dtag
                      wait("vector", SE, dtag)
                      wait("vector", GP, tags.get(("rst", hi - 2), 0))
                      rtag = op("vector", lambda e, hp=hi % 2: e.reciprocal(
                          rec_sb[hp][:], den_sb[hp][:]), VE)
                      tags[("rec", hi)] = rtag
                      wait("gpsimd", VE, rtag)
                      rst = dma("gpsimd", den_h[qc, head:head + 1, :], rec_sb[hi % 2][:], GP)
                      tags[("rst", hi)] = rst
                      wait("gpsimd", GP, rst)
                      wait("gpsimd", VE, tags.get(("otn", hi - 2), 0))
                      rbl = dma("gpsimd", recb[hi % 2][:],
                                den_h[qc, head:head + 1, :].partition_broadcast(128), GP)
                      tags[("rbl", hi)] = rbl
                      wait("vector", GP, rbl)
                      wait("vector", TE, hdone)
                      wait("vector", ST, tags.get(("otstore", qc - 2), 0))
                      otn = op("vector", lambda e, bo=bo, head=head, oq=qc % 2, hp=hi % 2: e.tensor_tensor(
                          ot_st[oq][:, head, :], ps_o[bo][:], recb[hp][:], ALU.mult), VE)
                      tags[("otn", hi)] = otn
                      hi += 1
                  tags[("kidone", ki)] = TE.n
              wait("scalar", VE, tags[("otn", hi - 1)])
              tags[("otstore", qc)] = dma(
                  "scalar", OT_h[:, :, qc * 512:(qc + 1) * 512].rearrange("t p r -> p t r"),
                  ot_st[qc % 2][:], ST)
              tags[("qcdone", qc)] = TE.n
          wait("sync", ST, tags[("otstore", 3)])
          barrier()
    _phase_atn()
    if upto < 7:
        return finish(nc, plan, ST)

    def _phase_oproj():
      with contextlib.ExitStack() as ph:
          _sb = lambda n, s, d: ph.enter_context(nc.sbuf_tensor(n, s, d))
          _ps = lambda n, s, d=F32: ph.enter_context(nc.psum_tensor(n, s, d))
          ot_sb = _sb("o_ot", [128, QH, RO], BF16)
          wo_sb = [_sb(f"o_w{i}", [128, HT, 128], BF16) for i in range(2)]
          yt_sb = [_sb(f"o_yt{i}", [128, 512], BF16) for i in range(2)]
          ps_y = [_ps(f"o_ps{i}", [128, 512]) for i in range(2)]
          lo = dma("sync", ot_sb[:], OT_h[:].rearrange("t p r -> p t r"), LD)
          wait("tensor", LD, lo)
          tags = {}
          g = 0
          for ho in range(HT):
              p = ho % 2
              wait("gpsimd", TE, tags.get(("ymm", ho - 2, 3), 0))
              gw = dma("gpsimd", wo_sb[p][:],
                       woT[:, ho * 128:(ho + 1) * 128].rearrange("(t p) d -> p t d", p=128), GP)
              wait("tensor", GP, gw)
              for rc in range(4):
                  bk = g % 2
                  wait("tensor", SE, tags.get(("yevac", g - 2), 0))

                  def ymm(e, p=p, rc=rc, bk=bk):
                      last = None
                      for dt in range(QH):
                          last = e.matmul(ps_y[bk][:], wo_sb[p][:, dt, :],
                                          ot_sb[:, dt, rc * 512:(rc + 1) * 512],
                                          start=(dt == 0), stop=(dt == QH - 1))
                      return last
                  tmm = op("tensor", ymm, TE)
                  tags[("ymm", ho, rc)] = tmm
                  wait("scalar", TE, tmm)
                  wait("scalar", ST, tags.get(("yst", g - 2), 0))
                  sev = op("scalar", lambda e, bk=bk: e.copy(yt_sb[bk][:], ps_y[bk][:]), SE)
                  tags[("yevac", g)] = sev
                  tags[("yst", g)] = dma("scalar", YT_h[ho, :, rc * 512:(rc + 1) * 512],
                                         yt_sb[bk][:], ST)
                  g += 1
          wait("sync", ST, tags[("yst", g - 1)])
          barrier()
    _phase_oproj()
    if upto < 8:
        return finish(nc, plan, ST)

    def _phase_o2():
      with contextlib.ExitStack() as ph:
          _sb = lambda n, s, d: ph.enter_context(nc.sbuf_tensor(n, s, d))
          _ps = lambda n, s, d=F32: ph.enter_context(nc.psum_tensor(n, s, d))
          ytc = [_sb(f"o2_yt{i}", [128, HT, 512], BF16) for i in range(2)]
          xb = [_sb(f"o2_x{i}", [128, D], F32) for i in range(2)]
          y1b = [_sb(f"o2_y1{i}", [128, D], F32) for i in range(2)]
          sq_scr = _sb("o2_sq", [128, D], F32)
          stat = [_sb(f"o2_st{i}", [128, 8], F32) for i in range(2)]
          lny = [_sb(f"o2_lny{i}", [128, D], BF16) for i in range(2)]
          lst = [_sb(f"o2_ls{i}", [128, HT, 128], BF16) for i in range(2)]
          ps_nat = _ps("o2_nat", [128, D], BF16)
          ps_tr = [_ps(f"o2_ptr{i}", [128, 512], BF16) for i in range(2)]
          tags = {}
          for rc in range(4):
              pc_ = rc % 2
              wait("sync", TE, tags.get(("natdone", rc - 2), 0))
              yl = dma("sync", ytc[pc_][:],
                       YT_h[:, :, rc * 512:(rc + 1) * 512].rearrange("t p r -> p t r"), LD)
              wait("tensor", LD, yl)
              for rsub in range(4):
                  i = rc * 4 + rsub
                  pi = i % 2
                  wait("sync", VE, tags.get(("y1", i - 2), 0))
                  lx = dma("sync", xb[pi][:], x_own[i * 128:(i + 1) * 128, :], LD)
                  wait("tensor", VE, tags.get(("y1", i - 1), 0))

                  def natf(e, pc_=pc_, rsub=rsub):
                      last = None
                      for ho in range(HT):
                          last = e.matmul(ps_nat[:, ho * 128:(ho + 1) * 128],
                                          ytc[pc_][:, ho, rsub * 128:(rsub + 1) * 128], ident[:],
                                          start=(ho % 8 == 0), stop=(ho % 8 == 7),
                                          is_transpose=True, skip_group_check=True)
                      return last
                  nt = op("tensor", natf, TE)
                  tags[("nat", i)] = nt
                  wait("vector", TE, nt)
                  wait("vector", LD, lx)
                  wait("vector", ST, tags.get(("y1st", i - 2), 0))
                  vy = op("vector", lambda e, pi=pi: e.tensor_tensor(
                      y1b[pi][:], ps_nat[:], xb[pi][:], ALU.add), VE)
                  tags[("y1", i)] = vy
                  wait("scalar", VE, vy)
                  tags[("y1st", i)] = dma("scalar", y1_h[i * 128:(i + 1) * 128, :], y1b[pi][:], ST)
                  wait("scalar", LD, lx)
                  vl = emit_ln_tile(y1b[pi], stat[pi], lny[pi], sq_scr, tags, 100 + i, 100 + i - 2)
                  wait("tensor", ST, tags.get(("lnyst", i - 2), 0))
                  emit_transpose_tile(lny[pi], lst[pi], 0, ps_tr, tags, 1000 + i * 8, vl)
                  lastev = tags[("trev", 1000 + i * 8 + 7)]
                  wait("scalar", SE, lastev)
                  tags[("lnyst", i)] = dma(
                      "scalar",
                      lnyT_h[:, :, i * 128:(i + 1) * 128].rearrange("t p r -> p t r"),
                      lst[pi][:], ST)
              tags[("natdone", rc)] = TE.n
          wait("sync", ST, tags[("lnyst", 15)])
          barrier()
    _phase_o2()
    if upto < 9:
        return finish(nc, plan, ST)

    def _phase_gate():
      with contextlib.ExitStack() as ph:
          _sb = lambda n, s, d: ph.enter_context(nc.sbuf_tensor(n, s, d))
          _ps = lambda n, s, d=F32: ph.enter_context(nc.psum_tensor(n, s, d))
          lnyT = _sb("g_lny", [128, HT, RO], BF16)
          wg_sb = [_sb(f"g_wg{i}", [128, HT, 128], BF16) for i in range(2)]
          wu_sb = [_sb(f"g_wu{i}", [128, HT, 128], BF16) for i in range(2)]
          s1_sb = [_sb(f"g_s1{i}", [128, 512], BF16) for i in range(2)]
          h_st = [_sb(f"g_h{i}", [128, RO], BF16) for i in range(2)]
          ps_g = [_ps(f"g_pg{i}", [128, 512]) for i in range(2)]
          ps_u = [_ps(f"g_pu{i}", [128, 512]) for i in range(2)]
          ll = dma("sync", lnyT[:], lnyT_h[:].rearrange("t p r -> p t r"), LD)
          wait("tensor", LD, ll)
          tags = {}
          for t in range(TD):
              p = t % 2
              wait("sync", TE, tags.get(("gdone", t - 2), 0))
              dma("sync", wg_sb[p][:], Wgu_h[t].rearrange("(h p) i -> p h i", p=128), LD)
              lw = dma("sync", wu_sb[p][:], Wgu_h[t + TD].rearrange("(h p) i -> p h i", p=128), LD)
              wait("tensor", LD, lw)
              for rc in range(4):
                  g = t * 4 + rc
                  bk = g % 2
                  wait("tensor", VE, tags.get(("hm", g - 2), 0))

                  def gmm(e, p=p, rc=rc, bk=bk):
                      last = None
                      for ht in range(HT):
                          last = e.matmul(ps_g[bk][:], wg_sb[p][:, ht, :],
                                          lnyT[:, ht, rc * 512:(rc + 1) * 512],
                                          start=(ht == 0), stop=(ht == HT - 1))
                      for ht in range(HT):
                          last = e.matmul(ps_u[bk][:], wu_sb[p][:, ht, :],
                                          lnyT[:, ht, rc * 512:(rc + 1) * 512],
                                          start=(ht == 0), stop=(ht == HT - 1),
                                          skip_group_check=True)
                      return last
                  tmm = op("tensor", gmm, TE)
                  wait("scalar", TE, tmm)
                  wait("scalar", VE, tags.get(("hm", g - 2), 0))
                  stag = op("scalar", lambda e, bk=bk: e.activation(
                      s1_sb[bk][:], ps_g[bk][:], AF.Silu), SE)
                  wait("vector", SE, stag)
                  if rc == 0:
                      wait("vector", ST, tags.get(("hst", t - 2), 0))
                  hm = op("vector", lambda e, bk=bk, p=p, rc=rc: e.tensor_tensor(
                      h_st[p][:, rc * 512:(rc + 1) * 512], ps_u[bk][:], s1_sb[bk][:],
                      ALU.mult), VE)
                  tags[("hm", g)] = hm
              wait("scalar", VE, tags[("hm", t * 4 + 3)])
              tags[("hst", t)] = dma("scalar", H_h[t, :, :], h_st[p][:], ST)
              tags[("gdone", t)] = TE.n
          wait("sync", ST, tags[("hst", TD - 1)])
          barrier()
    _phase_gate()
    if upto < 10:
        return finish(nc, plan, ST)

    def _phase_down():
      with contextlib.ExitStack() as ph:
          _sb = lambda n, s, d: ph.enter_context(nc.sbuf_tensor(n, s, d))
          _ps = lambda n, s, d=F32: ph.enter_context(nc.psum_tensor(n, s, d))
          hc = _sb("d_hc", [128, TD, 512], BF16)
          wdh = [_sb(f"d_w{i}", [128, TD, 128], BF16) for i in range(2)]
          dstage = _sb("d_ds", [128, HT, 512], BF16)
          y1b = [_sb("d_y10", [128, D], F32)]
          ob = [_sb("d_o0", [128, D], F32)]
          ps_d = [_ps(f"d_pd{i}", [128, 512]) for i in range(2)]
          ps_nat = _ps("d_nat", [128, D], BF16)
          tags = {}
          for rc in range(4):
              wait("sync", TE, tags.get(("mmdone", rc - 1), 0))
              hl = dma("sync", hc[:], H_h[:, :, rc * 512:(rc + 1) * 512].rearrange("t p r -> p t r"), LD)
              for ho in range(HT):
                  p = ho % 2
                  g = rc * HT + ho
                  wait("sync", TE, tags.get(("dmm", g - 2), 0))
                  lw = dma("sync", wdh[p][:],
                           Wdn_h[:, :, ho * 128:(ho + 1) * 128].rearrange("t p h -> p t h"), LD)
                  wait("tensor", LD, lw)
                  if ho == 0:
                      wait("tensor", LD, hl)
                  wait("tensor", SE, tags.get(("devac", g - 2), 0))
                  if ho == 0:
                      wait("scalar", TE, tags.get(("natlast", rc - 1), 0))

                  def dmm(e, p=p, bk=g % 2):
                      last = None
                      for t in range(TD):
                          last = e.matmul(ps_d[bk][:], wdh[p][:, t, :], hc[:, t, :],
                                          start=(t == 0), stop=(t == TD - 1))
                      return last
                  tmm = op("tensor", dmm, TE)
                  tags[("dmm", g)] = tmm
                  wait("scalar", TE, tmm)
                  sev = op("scalar", lambda e, ho=ho, bk=g % 2: e.copy(
                      dstage[:, ho, :], ps_d[bk][:]), SE)
                  tags[("devac", g)] = sev
              tags[("devac-last", rc)] = tags[("devac", rc * HT + HT - 1)]
              tags[("mmdone", rc)] = TE.n
              wait("tensor", SE, tags[("devac-last", rc)])
              for rsub in range(4):
                  i = rc * 4 + rsub
                  pi = 0
                  wait("sync", VE, tags.get(("y2", i - 1), 0))
                  ly = dma("sync", y1b[pi][:], y1_h[i * 128:(i + 1) * 128, :], LD)
                  wait("tensor", VE, tags.get(("y2", i - 1), 0))

                  def natf(e, rsub=rsub):
                      last = None
                      for ho in range(HT):
                          last = e.matmul(ps_nat[:, ho * 128:(ho + 1) * 128],
                                          dstage[:, ho, rsub * 128:(rsub + 1) * 128], ident[:],
                                          start=(ho % 8 == 0), stop=(ho % 8 == 7),
                                          is_transpose=True, skip_group_check=True)
                      return last
                  nt = op("tensor", natf, TE)
                  if rsub == 3:
                      tags[("natlast", rc)] = nt
                  wait("vector", TE, nt)
                  wait("vector", LD, ly)
                  wait("vector", ST, tags.get(("ost", i - 1), 0))
                  vy = op("vector", lambda e, pi=pi: e.tensor_tensor(
                      ob[pi][:], ps_nat[:], y1b[pi][:], ALU.add), VE)
                  tags[("y2", i)] = vy
                  wait("scalar", VE, vy)
                  tags[("ost", i)] = dma("scalar", out[i * 128:(i + 1) * 128, :], ob[pi][:], ST)
          wait("sync", ST, tags[("ost", 15)])
          barrier()

    _phase_down()
    return finish(nc, plan, ST)


def finish(nc, plan, ST):
    with nc.Block() as block:
        @block.sync
        def _(eng):
            for fn in plan.ops["sync"]:
                fn(eng)

        @block.scalar
        def _(eng):
            for fn in plan.ops["scalar"]:
                fn(eng)

        @block.vector
        def _(eng):
            for fn in plan.ops["vector"]:
                fn(eng)

        @block.tensor
        def _(eng):
            for fn in plan.ops["tensor"]:
                fn(eng)

        @block.gpsimd
        def _(eng):
            for fn in plan.ops["gpsimd"]:
                fn(eng)
    return nc



def _rope_tables():
    inv = 1.0 / (10000.0 ** (np.arange(0, 128, 2, dtype=np.float64) / 128.0))
    pos = np.arange(S, dtype=np.float64)
    fr = pos[:, None] * inv[None, :]
    emb = np.concatenate([fr, fr], axis=1)
    cos = np.cos(emb).astype(np.float32).T.copy()
    sin = np.sin(emb).astype(np.float32).T.copy()
    sin_hat = sin.copy()
    sin_hat[0:64] *= -1.0
    return cos, sin_hat


def _core_inputs(inputs):
    x = np.asarray(inputs["L0_attn_hidden_states"], np.float32)
    wq = np.asarray(inputs["L0_attn_Wq"], np.float32)
    wk = np.asarray(inputs["L0_attn_Wk"], np.float32)
    wv = np.asarray(inputs["L0_attn_Wv"], np.float32)
    wo = np.asarray(inputs["L0_attn_Wo"], np.float32)
    qwg = np.asarray(inputs["L0_ffn_qweight_gu"], np.int32)
    qzg = np.asarray(inputs["L0_ffn_qzeros_gu"], np.int32)
    scg = np.asarray(inputs["L0_ffn_scales_gu"], np.float32)
    qwd = np.asarray(inputs["L0_ffn_qweight_dn"], np.int32)
    qzd = np.asarray(inputs["L0_ffn_qzeros_dn"], np.int32)
    scd = np.asarray(inputs["L0_ffn_scales_dn"], np.float32)

    shared = {
        "wqT": np.ascontiguousarray(wq.T[PH_PERM]),
        "wkT": np.ascontiguousarray(wk.T[PH_PERM]),
        "wvT": np.ascontiguousarray(wv.T[PH_PERM]),
        "woT": np.ascontiguousarray(wo.T[:, PH_PERM]),
        "qw_gu": np.ascontiguousarray(qwg[PH_PERM]),
        "qz_gu": qzg,
        "sc_gu": np.ascontiguousarray(scg[:, PI_PERM]),
        "qw_dn": np.ascontiguousarray(qwd[PI_DN]),
        "qz_dn": qzd,
        "sc_dn": np.ascontiguousarray(scd[:, PH_PERM]),
    }
    cos, sin_hat = _rope_tables()
    shared["cos_k"] = cos
    shared["sin_k"] = sin_hat

    maps = []
    for c in range(8):
        b, half = c // 2, c % 2
        chunks = CHUNKS[half]
        xb = np.ascontiguousarray(x[b][:, PH_PERM])
        rows = np.concatenate([np.arange(ch * 512, (ch + 1) * 512) for ch in chunks])
        m = dict(shared)
        m["x_kv"] = xb
        m["x_own"] = np.ascontiguousarray(xb[rows])
        m["cos_q"] = np.ascontiguousarray(cos[:, rows])
        m["sin_q"] = np.ascontiguousarray(sin_hat[:, rows])
        bias = np.zeros((128, 4, 32), np.float32)
        for j, ch in enumerate(chunks):
            nst_true = (ch + 1) * 4
            bias[:, j, nst_true:] = NEG
        m["biascfg"] = bias
        dm = np.zeros((128, 4, 8, 512), np.float32)
        p = np.arange(128)
        r = np.arange(512)
        for j, ch in enumerate(chunks):
            for site in range(8):
                st = 8 * j + site
                sglob = st * 128 + p
                rglob = ch * 512 + r
                dm[:, j, site, :] = (sglob[:, None] <= rglob[None, :]).astype(np.float32)
        m["dmask"] = dm.astype(ml_dtypes.bfloat16)
        maps.append(m)
    return maps


_prog_cache = {}


def kernel(**inputs):
    maps = _core_inputs(inputs)
    key = (DBG_FLAGS["dbg"], DBG_FLAGS["upto"])
    if key not in _prog_cache:
        _prog_cache[key] = build_program(dbg=key[0], upto=key[1])
    nc = _prog_cache[key]
    res = run_bass_kernel_spmd(nc, maps, core_ids=list(range(8)))
    if DBG_FLAGS["dbg"]:
        kernel.last_results = res.results
    y = np.empty((4, S, D), np.float32)
    for c in range(8):
        b, half = c // 2, c % 2
        o = np.asarray(res.results[c]["out"])
        for j, ch in enumerate(CHUNKS[half]):
            blk = np.empty((512, D), np.float32)
            blk[:, PH_PERM] = o[j * 512:(j + 1) * 512]
            y[b, ch * 512:(ch + 1) * 512] = blk
    return y


DBG_FLAGS = {"dbg": False, "upto": 99}


# version 03
# speedup vs baseline: 1.1937x; 1.1937x over previous
import contextlib
import numpy as np
import ml_dtypes
import concourse.bass as bass
import concourse.mybir as mybir
from concourse.bass_utils import run_bass_kernel_spmd

F32 = mybir.dt.float32
BF16 = mybir.dt.bfloat16
I32 = mybir.dt.int32
AF = mybir.ActivationFunctionType
ALU = mybir.AluOpType

D = 4096
S = 4096
RO = 2048
HT = 32
QH = 32
KVH = 8
INTER = 11008
I2 = 22016
PC = 2752
TG = 172
TD = 86
SBOUND = [1024, 2048, 3072, 4096]
NST = [8, 16, 24, 32]
CHUNKS = [[0, 3, 4, 7], [1, 2, 5, 6]]
EPS = 1e-5
SCALE = float(1.0 / np.sqrt(128.0))
NEG = -60000.0

ENGS = ("sync", "scalar", "vector", "tensor", "gpsimd")


def _perm_block(n):
    idx = np.arange(n)
    t, rem = idx // 128, idx % 128
    j, v = rem // 16, rem % 16
    return 128 * t + 8 * v + j


PH_PERM = _perm_block(D)
PI_PERM = _perm_block(I2)
PI_DN = _perm_block(INTER)


class Sem:
    def __init__(self, nc, es, name):
        self.h = es.enter_context(nc.semaphore(name))
        self.n = 0

    def add(self, k):
        self.n += k
        return self.n


class Plan:
    def __init__(self):
        self.ops = {e: [] for e in ENGS}

    def emit(self, eng, fn):
        self.ops[eng].append(fn)


def build_program(dbg=False, upto=99):
    nc = bass.Bass("TRN2", target_bir_lowering=False)
    es = contextlib.ExitStack()
    plan = Plan()
    skind = "ExternalOutput" if dbg else "Internal"

    def din(name, shape, dt):
        return nc.dram_tensor(name, shape, dt, kind="ExternalInput").ap()

    x_kv = din("x_kv", [S, D], F32)
    x_own = din("x_own", [RO, D], F32)
    wqT = din("wqT", [D, D], F32)
    wkT = din("wkT", [D, 1024], F32)
    wvT = din("wvT", [D, 1024], F32)
    woT = din("woT", [D, D], F32)
    cos_k = din("cos_k", [128, S], F32)
    sin_k = din("sin_k", [128, S], F32)
    cos_q = din("cos_q", [128, RO], F32)
    sin_q = din("sin_q", [128, RO], F32)
    qw_gu = din("qw_gu", [D, PC], I32)
    qz_gu = din("qz_gu", [32, PC], I32)
    sc_gu = din("sc_gu", [32, I2], F32)
    qw_dn = din("qw_dn", [INTER, 512], I32)
    qz_dn = din("qz_dn", [TD, 512], I32)
    sc_dn = din("sc_dn", [TD, D], F32)
    dmask = din("dmask", [128, 4, 8, 512], BF16)
    biascfg = din("biascfg", [128, 4, 32], F32)

    out = nc.dram_tensor("out", [RO, D], F32, kind="ExternalOutput").ap()

    def scr(name, shape, dt):
        return nc.dram_tensor(name, shape, dt, kind=skind).ap()

    zrow_gu = scr("zrow_gu", [32, I2], BF16)
    srow_gu = scr("srow_gu", [32, I2], BF16)
    zrow_dn = scr("zrow_dn", [TD, D], BF16)
    srow_dn = scr("srow_dn", [TD, D], BF16)
    Wgu_h = scr("Wgu_h", [D, I2], BF16)
    Wdn_h = scr("Wdn_h", [TD, 128, D], BF16)
    LNxT_h = scr("LNxT_h", [HT, 128, S], BF16)
    LNoT_h = scr("LNoT_h", [HT, 128, RO], BF16)
    KT_h = scr("KT_h", [KVH, 128, S], BF16)
    V_h = scr("V_h", [KVH, 32, 128, 128], BF16)
    QT_h = scr("QT_h", [QH, 128, RO], BF16)
    OT_h = scr("OT_h", [QH, 128, RO], BF16)
    YT_h = scr("YT_h", [HT, 128, RO], BF16)
    y1_h = scr("y1_h", [RO, D], F32)
    lnyT_h = scr("lnyT_h", [HT, 128, RO], BF16)
    H_h = scr("H_h", [TD, 128, RO], BF16)
    den_h = scr("den_h", [4, QH, 512], F32)

    ident_t = nc.inline_tensor(np.eye(128, dtype=ml_dtypes.bfloat16), name="identb")
    ones_t = nc.inline_tensor(np.ones((128, 1), dtype=ml_dtypes.bfloat16), name="onesb")

    LD = Sem(nc, es, "LD")
    ST = Sem(nc, es, "ST")
    GP = Sem(nc, es, "GP")
    TE = Sem(nc, es, "TE")
    VE = Sem(nc, es, "VE")
    SE = Sem(nc, es, "SE")
    PB = Sem(nc, es, "PB")

    phase_no = [0]

    def barrier():
        target = 5 * (phase_no[0] + 1)
        for e in ENGS:
            plan.emit(e, lambda eng: eng.sem_inc(PB.h, 1))
        for e in ENGS:
            plan.emit(e, lambda eng, t=target: eng.wait_ge(PB.h, t))
        phase_no[0] += 1
        PB.n = target

    def dma(queue, o, i, sem):
        def fn(eng, o=o, i=i, s=sem.h):
            eng.dma_start(out=o, in_=i).then_inc(s, 16)
        plan.emit(queue, fn)
        return sem.add(16)

    def wait(eng, sem, tag):
        if tag and tag > 0:
            plan.emit(eng, lambda e, s=sem.h, t=tag: e.wait_ge(s, t))

    def op(eng, fn, sem=None):
        if sem is None:
            plan.emit(eng, fn)
            return None
        tag = sem.add(1)

        def wrapped(e, fn=fn, s=sem.h):
            fn(e).then_inc(s, 1)
        plan.emit(eng, wrapped)
        return tag

    def gsb(name, shape, dt):
        return es.enter_context(nc.sbuf_tensor(name, shape, dt))

    ident = gsb("g_ident", [128, 128], BF16)
    ones = gsb("g_ones", [128, 1], BF16)
    bias_sb = gsb("g_bias", [128, 4, 32], F32)
    eps_sb = gsb("g_eps", [128, 1], F32)
    plan.emit("vector", lambda e: e.memset(eps_sb[:], EPS))
    dma("sync", ident[:], ident_t.ap(), LD)
    dma("sync", ones[:], ones_t.ap(), LD)
    gtag = dma("sync", bias_sb[:], biascfg[:], LD)
    for e in ("vector", "tensor", "scalar", "gpsimd"):
        wait(e, LD, gtag)

    def vfence(tag):
        wait("vector", VE, tag)

    def emit_ln_tile(xb, stat, lnx, sq_scr, tags, key, prev_key):
        t1 = op("vector", lambda e: e.reduce_sum(stat[:, 0:1], xb[:],
                                                 axis=mybir.AxisListType.X), VE)
        wait("scalar", VE, tags.get(("vvfree", prev_key), 0))
        stg = op("scalar", lambda e: e.activation(sq_scr[:], xb[:], AF.Square,
                                                  accum_out=stat[:, 1:2]), SE)
        wait("vector", SE, stg)
        vfence(t1)
        t2 = op("vector", lambda e: e.tensor_scalar(stat[:, 2:3], stat[:, 0:1], 1.0 / D,
                                                    None, ALU.mult), VE)
        t3 = op("vector", lambda e: e.tensor_scalar(stat[:, 3:4], stat[:, 1:2], 1.0 / D,
                                                    None, ALU.mult), VE)
        vfence(t2)
        t4 = op("vector", lambda e: e.tensor_tensor(stat[:, 4:5], stat[:, 2:3], stat[:, 2:3],
                                                    ALU.mult), VE)
        vfence(t4)
        vv = op("vector", lambda e: e.tensor_tensor(stat[:, 5:6], stat[:, 3:4], stat[:, 4:5],
                                                    ALU.subtract), VE)
        tags[("vvfree", key)] = vv
        wait("scalar", VE, vv)
        sst = op("scalar", lambda e: e.activation(stat[:, 6:7], stat[:, 5:6], AF.Sqrt,
                                                  bias=eps_sb[:]), SE)
        wait("vector", SE, sst)
        t5 = op("vector", lambda e: e.reciprocal(stat[:, 7:8], stat[:, 6:7]), VE)
        vfence(t5)
        vl = op("vector", lambda e: e.tensor_scalar(lnx[:], xb[:], stat[:, 2:3], stat[:, 7:8],
                                                    ALU.subtract, ALU.mult), VE)
        tags[("lnx", key)] = vl
        return vl

    def emit_transpose_tile(lnx, dst, rcol, ps_tr, tags, gidx_base, vl_tag):
        wait("tensor", VE, vl_tag)
        last = 0
        for g in range(8):
            bk = (gidx_base + g) % 2
            wait("tensor", SE, tags.get(("trev", gidx_base + g - 2), 0))

            def trf(e, g=g, bk=bk, lnx=lnx):
                lastmm = None
                for u in range(4):
                    ht = g * 4 + u
                    lastmm = e.matmul(ps_tr[bk][:, u * 128:(u + 1) * 128],
                                      lnx[:, ht * 128:(ht + 1) * 128], ident[:],
                                      start=(u == 0), stop=(u == 3), is_transpose=True)
                return lastmm
            tt = op("tensor", trf, TE)
            wait("scalar", TE, tt)
            sev = op("scalar", lambda e, g=g, bk=bk, dst=dst, rcol=rcol: e.copy(
                dst[:, g * 4:(g + 1) * 4, rcol:rcol + 128],
                ps_tr[bk][:].rearrange("p (t r) -> p t r", r=128)), SE)
            tags[("trev", gidx_base + g)] = sev
            last = sev
        return last

    def emit_rope(psrc, cosb, sinb, coloff, rbufs, dst, tags=None):
        rot, t1, t2 = rbufs
        op("vector", lambda e: e.tensor_copy(rot[0:64, :], psrc[64:128, :]))
        op("vector", lambda e: e.tensor_copy(rot[64:128, :], psrc[0:64, :]))
        op("vector", lambda e: e.tensor_tensor(t1[:], psrc[:], cosb[:, coloff:coloff + 512],
                                               ALU.mult))
        op("vector", lambda e: e.tensor_tensor(t2[:], rot[:], sinb[:, coloff:coloff + 512],
                                               ALU.mult))
        return op("vector", lambda e: e.tensor_tensor(dst[:], t1[:], t2[:], ALU.add), VE)

    def _phase_w0():
      with contextlib.ExitStack() as ph:
          _sb = lambda n, s, d: ph.enter_context(nc.sbuf_tensor(n, s, d))
          with contextlib.ExitStack() as ph2:
              _sb2 = lambda n, s, d: ph2.enter_context(nc.sbuf_tensor(n, s, d))
              zq = _sb2("w0_zq", [32, PC], I32)
              zsm = _sb2("w0_zsm", [32, PC], I32)
              zb = _sb2("w0_zb", [32, I2], BF16)
              scf = _sb2("w0_scf", [32, INTER], F32)
              scb = _sb2("w0_scb", [32, I2], BF16)
              t1 = dma("sync", zq[:], qz_gu[:], LD)
              t2a = dma("sync", scf[:], sc_gu[:, 0:INTER], LD)
              wait("vector", LD, t2a)
              for j in range(8):
                  op("vector", lambda e, j=j: e.tensor_scalar(
                      zsm[:], zq[:], 4 * j, 15, ALU.logical_shift_right, ALU.bitwise_and))
                  op("vector", lambda e, j=j: e.tensor_copy(
                      zb[:].rearrange("g (t b v) -> g t b v", t=TG, b=8, v=16)[:, :, j, :],
                      zsm[:].rearrange("g (t v) -> g t v", v=16)))
              v1 = op("vector", lambda e: e.tensor_copy(scb[:, 0:INTER], scf[:]), VE)
              wait("sync", VE, v1)
              t2b = dma("sync", scf[:], sc_gu[:, INTER:I2], LD)
              wait("vector", LD, t2b)
              v2 = op("vector", lambda e: e.tensor_copy(scb[:, INTER:I2], scf[:]), VE)
              wait("scalar", VE, v2)
              dma("scalar", zrow_gu[:], zb[:], ST)
              stg = dma("scalar", srow_gu[:], scb[:], ST)
              wait("sync", ST, stg)
          zq2 = _sb("w0_zq2", [TD, 512], I32)
          zsm2 = _sb("w0_zsm2", [TD, 512], I32)
          zb2 = _sb("w0_zb2", [TD, D], BF16)
          scf2 = _sb("w0_scf2", [TD, D], F32)
          scb2 = _sb("w0_scb2", [TD, D], BF16)
          t3 = dma("sync", zq2[:], qz_dn[:], LD)
          t4 = dma("sync", scf2[:], sc_dn[:], LD)
          wait("vector", LD, t4)
          for j in range(8):
              op("vector", lambda e, j=j: e.tensor_scalar(
                  zsm2[:], zq2[:], 4 * j, 15, ALU.logical_shift_right, ALU.bitwise_and))
              op("vector", lambda e, j=j: e.tensor_copy(
                  zb2[:].rearrange("g (T b v) -> g T b v", T=HT, b=8, v=16)[:, :, j, :],
                  zsm2[:].rearrange("g (T v) -> g T v", v=16)))
          vt = op("vector", lambda e: e.tensor_copy(scb2[:], scf2[:]), VE)
          wait("scalar", VE, vt)
          dma("scalar", zrow_dn[:], zb2[:], ST)
          stg = dma("scalar", srow_dn[:], scb2[:], ST)
          wait("sync", ST, stg)
          barrier()
    _phase_w0()
    if upto < 1:
        return finish(nc, plan, ST)

    def make_w1_quanta(_sb):
        NQ = 4
        QP = PC // NQ
        QI = QP * 8
        WW = QI
        qwb = [_sb(f"w1_qw{i}", [128, QP], I32) for i in range(2)]
        wint = _sb("w1_wint", [128, 352 * 8], I32)
        wd = [_sb(f"w1_wd{i}", [128, WW], BF16) for i in range(2)]
        zbc = [_sb(f"w1_zb{i}", [128, WW], BF16) for i in range(2)]
        sbc = [_sb(f"w1_sb{i}", [128, WW], BF16) for i in range(2)]
        tags = {}

        def w1_quant(ltag, g2, p, npc, n_i, store_fn, it):
            wait("vector", GP, max(g2, ltag))
            wait("vector", GP, tags.get(("st", it - 2), 0))
            splits = [(0, 352), (352, npc - 352)] if npc > 352 else [(0, npc)]
            for off, w in splits:
                nb = w * 8 // 128
                ut = 0
                for j in range(8):
                    ut = op("vector", lambda e, j=j, p=p, nb=nb, off=off, w=w: e.tensor_scalar(
                        wint[:, 0:nb * 128].rearrange("p (t b v) -> p t b v", t=nb, b=8, v=16)[:, :, j, :],
                        qwb[p][:, off:off + w].rearrange("p (t v) -> p t v", v=16),
                        4 * j, 15, ALU.logical_shift_right, ALU.bitwise_and), VE if j == 7 else None)
                wait("vector", VE, ut)
                op("vector", lambda e, p=p, off=off, nb=nb: e.tensor_copy(
                    wd[p][:, off * 8:off * 8 + nb * 128], wint[:, 0:nb * 128]))
            op("vector", lambda e, p=p, n=n_i: e.tensor_tensor(
                wd[p][:, 0:n], wd[p][:, 0:n], zbc[p][:, 0:n], ALU.subtract))
            vt = op("vector", lambda e, p=p, n=n_i: e.tensor_tensor(
                wd[p][:, 0:n], wd[p][:, 0:n], sbc[p][:, 0:n], ALU.mult), VE)
            tags[("vfree", it)] = vt
            wait("gpsimd", VE, vt)
            tags[("st", it)] = store_fn()

        quanta = []

        def mk_gu(ht, q, it):
            def emit():
                p = it % 2
                wait("gpsimd", VE, tags.get(("vfree", it - 2), 0))
                ltag = dma("gpsimd", qwb[p][:, 0:QP],
                           qw_gu[ht * 128:(ht + 1) * 128, q * QP:(q + 1) * QP], GP)
                dma("gpsimd", zbc[p][:, 0:QI],
                    zrow_gu[ht:ht + 1, q * QI:(q + 1) * QI].partition_broadcast(128), GP)
                g2 = dma("gpsimd", sbc[p][:, 0:QI],
                         srow_gu[ht:ht + 1, q * QI:(q + 1) * QI].partition_broadcast(128), GP)

                def store(p=p, ht=ht, q=q):
                    return dma("gpsimd",
                               Wgu_h[ht * 128:(ht + 1) * 128, q * QI:(q + 1) * QI],
                               wd[p][:, 0:QI], GP)
                w1_quant(ltag, g2, p, QP, QI, store, it)
            return emit

        def mk_dn(t, it):
            def emit():
                p = it % 2
                wait("gpsimd", VE, tags.get(("vfree", it - 2), 0))
                ltag = dma("gpsimd", qwb[p][:, 0:512], qw_dn[t * 128:(t + 1) * 128, :], GP)
                dma("gpsimd", zbc[p][:, 0:D], zrow_dn[t:t + 1, :].partition_broadcast(128), GP)
                g2 = dma("gpsimd", sbc[p][:, 0:D], srow_dn[t:t + 1, :].partition_broadcast(128), GP)

                def store(p=p, t=t):
                    return dma("gpsimd", Wdn_h[t, :, :], wd[p][:, 0:D], GP)
                w1_quant(ltag, g2, p, 512, D, store, it)
            return emit

        it = 0
        for ht in range(HT):
            for q in range(4):
                quanta.append(mk_gu(ht, q, it))
                it += 1
        for t in range(TD):
            quanta.append(mk_dn(t, it))
            it += 1
        return quanta

    def _phase_w1_standalone():
      with contextlib.ExitStack() as ph:
          _sb = lambda n, s, d: ph.enter_context(nc.sbuf_tensor(n, s, d))
          for fn in make_w1_quanta(_sb):
              fn()
          barrier()
    if upto < 7:
        _phase_w1_standalone()
    if upto < 2:
        return finish(nc, plan, ST)

    def emit_kv_phase(which):
        with contextlib.ExitStack() as ph:
            _sb = lambda n, s, d: ph.enter_context(nc.sbuf_tensor(n, s, d))
            _ps = lambda n, s, d=F32: ph.enter_context(nc.psum_tensor(n, s, d))
            w_sb = _sb(f"{which}_w", [128, HT, 1024], BF16)
            gw = dma("gpsimd", w_sb[:],
                     (wkT if which == "k" else wvT)[:].rearrange("(t p) d -> p t d", p=128), GP)
            ps_mm = [_ps(f"{which}_pm{i}", [128, 512]) for i in range(2)]
            tags = {}
            if which == "k":
                lnxT = _sb("k_lnxT", [128, HT, 512], BF16)
                xb = [_sb(f"k_x{i}", [128, D], F32) for i in range(2)]
                sq_scr = _sb("k_sq", [128, D], F32)
                stat = [_sb(f"k_st{i}", [128, 8], F32) for i in range(2)]
                lnx = [_sb(f"k_lnx{i}", [128, D], BF16) for i in range(2)]
                ps_tr = [_ps(f"k_ptr{i}", [128, 512], BF16) for i in range(2)]
                cosb = [_sb(f"k_cos{i}", [128, 512], F32) for i in range(2)]
                sinb = [_sb(f"k_sin{i}", [128, 512], F32) for i in range(2)]
                rope_b = [_sb(f"k_r{u}", [128, 512], F32) for u in range(3)]
                kt_sb = [_sb(f"k_kt{i}", [128, 512], BF16) for i in range(2)]
            else:
                lnxTb = [_sb(f"v_lnxT{i}", [128, HT, 512], BF16) for i in range(2)]
                v_sb = [_sb(f"v_o{i}", [128, 512], BF16) for i in range(2)]
            wait("tensor", GP, gw)
            for sc in range(8):
                if which == "k":
                    pcs = sc % 2
                    wait("sync", VE, tags.get(("rope", (sc - 2) * 8 + 7), 0))
                    dma("sync", cosb[pcs][:], cos_k[:, sc * 512:(sc + 1) * 512], LD)
                    cstag = dma("sync", sinb[pcs][:], sin_k[:, sc * 512:(sc + 1) * 512], LD)
                    tags[("cosld", sc)] = cstag
                    for rt in range(4):
                        i = sc * 4 + rt
                        pi = i % 2
                        wait("sync", VE, tags.get(("lnx", i - 2), 0))
                        lx = dma("sync", xb[pi][:], x_kv[i * 128:(i + 1) * 128, :], LD)
                        wait("vector", LD, lx)
                        wait("scalar", LD, lx)
                        vl = emit_ln_tile(xb[pi], stat[pi], lnx[pi], sq_scr, tags, i, i - 2)
                        if sc > 0 and rt == 0:
                            wait("tensor", ST, tags[("lnspill", sc - 1)])
                            wait("scalar", TE, tags[("kmms", sc - 1)])
                        emit_transpose_tile(lnx[pi], lnxT, rt * 128, ps_tr, tags, i * 8, vl)
                    lastev = tags[("trev", (sc * 4 + 3) * 8 + 7)]
                    wait("scalar", SE, lastev)
                    tags[("lnspill", sc)] = dma(
                        "scalar", LNxT_h[:, :, sc * 512:(sc + 1) * 512].rearrange("t p s -> p t s"),
                        lnxT[:], ST)
                    wait("tensor", SE, lastev)
                    for dt in range(8):
                        kg = sc * 8 + dt
                        bk = kg % 2
                        wait("tensor", VE, tags.get(("rope", kg - 2), 0))

                        def kmm(e, dt=dt, bk=bk):
                            last = None
                            for ht in range(HT):
                                last = e.matmul(ps_mm[bk][:], w_sb[:, ht, dt * 128:(dt + 1) * 128],
                                                lnxT[:, ht, :], start=(ht == 0), stop=(ht == HT - 1))
                            return last
                        tmm = op("tensor", kmm, TE)
                        wait("vector", TE, tmm)
                        wait("vector", ST, tags.get(("ktst", kg - 2), 0))
                        if dt == 0:
                            wait("vector", LD, tags[("cosld", sc)])
                        rtag = emit_rope(ps_mm[bk], cosb[pcs], sinb[pcs], 0, rope_b, kt_sb[bk])
                        tags[("rope", kg)] = rtag
                        wait("scalar", VE, rtag)
                        tags[("ktst", kg)] = dma(
                            "scalar", KT_h[dt, :, sc * 512:(sc + 1) * 512], kt_sb[bk][:], ST)
                    tags[("kmms", sc)] = TE.n
                else:
                    pc_ = sc % 2
                    wait("sync", TE, tags.get(("vmms", sc - 2), 0))
                    lt = dma("sync", lnxTb[pc_][:],
                             LNxT_h[:, :, sc * 512:(sc + 1) * 512].rearrange("t p s -> p t s"), LD)
                    wait("tensor", LD, lt)
                    for sl in range(4):
                        for dc in range(2):
                            g = sc * 8 + sl * 2 + dc
                            bk = g % 2
                            wait("tensor", SE, tags.get(("vevac", g - 2), 0))

                            def vmm(e, sl=sl, dc=dc, bk=bk, pc_=pc_):
                                last = None
                                for ht in range(HT):
                                    last = e.matmul(ps_mm[bk][:],
                                                    lnxTb[pc_][:, ht, sl * 128:(sl + 1) * 128],
                                                    w_sb[:, ht, dc * 512:(dc + 1) * 512],
                                                    start=(ht == 0), stop=(ht == HT - 1))
                                return last
                            tmm = op("tensor", vmm, TE)
                            wait("scalar", TE, tmm)
                            wait("scalar", ST, tags.get(("vst", g - 2), 0))
                            sev = op("scalar", lambda e, bk=bk: e.copy(v_sb[bk][:], ps_mm[bk][:]), SE)
                            tags[("vevac", g)] = sev
                            tags[("vst", g)] = dma(
                                "scalar",
                                V_h[dc * 4:(dc + 1) * 4, sc * 4 + sl, :, :].rearrange("k p d -> p k d"),
                                v_sb[bk][:].rearrange("p (k d) -> p k d", d=128), ST)
                    tags[("vmms", sc)] = TE.n
            last_st = max(v for k, v in tags.items() if k[0] in ("ktst", "vst", "lnspill"))
            wait("sync", ST, last_st)
        barrier()

    emit_kv_phase("k")
    if upto < 3:
        return finish(nc, plan, ST)
    emit_kv_phase("v")
    if upto < 4:
        return finish(nc, plan, ST)

    def _phase_q1():
      with contextlib.ExitStack() as ph:
          _sb = lambda n, s, d: ph.enter_context(nc.sbuf_tensor(n, s, d))
          _ps = lambda n, s, d=F32: ph.enter_context(nc.psum_tensor(n, s, d))
          lnoT = _sb("q1_lnoT", [128, HT, 512], BF16)
          xb = [_sb(f"q1_x{i}", [128, D], F32) for i in range(2)]
          sq_scr = _sb("q1_sq", [128, D], F32)
          stat = [_sb(f"q1_st{i}", [128, 8], F32) for i in range(2)]
          lnx = [_sb(f"q1_lnx{i}", [128, D], BF16) for i in range(2)]
          ps_tr = [_ps(f"q1_ptr{i}", [128, 512], BF16) for i in range(2)]
          tags = {}
          for sc in range(4):
              for rt in range(4):
                  i = sc * 4 + rt
                  pi = i % 2
                  wait("sync", VE, tags.get(("lnx", i - 2), 0))
                  lx = dma("sync", xb[pi][:], x_own[i * 128:(i + 1) * 128, :], LD)
                  wait("vector", LD, lx)
                  wait("scalar", LD, lx)
                  vl = emit_ln_tile(xb[pi], stat[pi], lnx[pi], sq_scr, tags, i, i - 2)
                  if sc > 0 and rt == 0:
                      wait("tensor", ST, tags[("spill", sc - 1)])
                  emit_transpose_tile(lnx[pi], lnoT, rt * 128, ps_tr, tags, i * 8, vl)
              lastev = tags[("trev", (sc * 4 + 3) * 8 + 7)]
              wait("scalar", SE, lastev)
              tags[("spill", sc)] = dma(
                  "scalar", LNoT_h[:, :, sc * 512:(sc + 1) * 512].rearrange("t p s -> p t s"),
                  lnoT[:], ST)
          wait("sync", ST, tags[("spill", 3)])
          barrier()
    _phase_q1()
    if upto < 5:
        return finish(nc, plan, ST)

    def _phase_q2():
      with contextlib.ExitStack() as ph:
          _sb = lambda n, s, d: ph.enter_context(nc.sbuf_tensor(n, s, d))
          _ps = lambda n, s, d=F32: ph.enter_context(nc.psum_tensor(n, s, d))
          lnq = _sb("q2_lnq", [128, HT, RO], BF16)
          wq_sb = [_sb(f"q2_w{i}", [128, HT, 128], BF16) for i in range(2)]
          cosb = _sb("q2_cos", [128, RO], F32)
          sinb = _sb("q2_sin", [128, RO], F32)
          rope_b = [_sb(f"q2_r{u}", [128, 512], F32) for u in range(3)]
          qt_st = [_sb(f"q2_qt{i}", [128, RO], BF16) for i in range(2)]
          ps_q = [_ps(f"q2_ps{i}", [128, 512]) for i in range(8)]
          dma("sync", cosb[:], cos_q[:], LD)
          dma("sync", sinb[:], sin_q[:], LD)
          lq = dma("sync", lnq[:], LNoT_h[:].rearrange("t p s -> p t s"), LD)
          wait("tensor", LD, lq)
          tags = {}
          for dt in range(QH):
              p = dt % 2
              wait("gpsimd", TE, tags.get(("qmm", dt - 2), 0))
              gw = dma("gpsimd", wq_sb[p][:],
                       wqT[:, dt * 128:(dt + 1) * 128].rearrange("(t p) d -> p t d", p=128), GP)
              wait("tensor", GP, gw)
              wait("tensor", VE, tags.get(("rope", dt - 2, 3), 0))

              def qmm(e, p=p, dt=dt):
                  last = None
                  for ht in range(HT):
                      for rc in range(4):
                          last = e.matmul(ps_q[(dt % 2) * 4 + rc][:], wq_sb[p][:, ht, :],
                                          lnq[:, ht, rc * 512:(rc + 1) * 512],
                                          start=(ht == 0), stop=(ht == HT - 1),
                                          skip_group_check=True)
                  return last
              tmm = op("tensor", qmm, TE)
              tags[("qmm", dt)] = tmm
              wait("vector", TE, tmm)
              wait("vector", ST, tags.get(("qtst", dt - 2), 0))
              for rc in range(4):
                  rtag = emit_rope(ps_q[(dt % 2) * 4 + rc], cosb, sinb, rc * 512, rope_b,
                                   qt_st[p][:, rc * 512:(rc + 1) * 512].rearrange("p r -> p r"))
                  tags[("rope", dt, rc)] = rtag
              wait("scalar", VE, tags[("rope", dt, 3)])
              tags[("qtst", dt)] = dma("scalar", QT_h[dt, :, :], qt_st[p][:], ST)
          wait("sync", ST, tags[("qtst", QH - 1)])
          barrier()
    _phase_q2()
    if upto < 6:
        return finish(nc, plan, ST)

    def _phase_atn():
      with contextlib.ExitStack() as ph:
          _sb = lambda n, s, d: ph.enter_context(nc.sbuf_tensor(n, s, d))
          _ps = lambda n, s, d=F32: ph.enter_context(nc.psum_tensor(n, s, d))
          qt_sb = _sb("a_qt", [128, QH, 512], BF16)
          kt_b = [_sb(f"a_kt{i}", [128, S], BF16) for i in range(2)]
          v_b = [_sb(f"a_v{i}", [128, 32, 128], BF16) for i in range(2)]
          NPT = 4
          pt = [_sb(f"a_pt{i}", [128, 512], BF16) for i in range(NPT)]
          dm_sb = [_sb("a_dm0", [128, 8, 512], BF16)]
          ot_st = _sb("a_ot", [128, QH, 512], BF16)
          den_sb = [_sb(f"a_dn{i}", [1, 512], F32) for i in range(2)]
          rec_sb = [_sb(f"a_rc{i}", [1, 512], F32) for i in range(2)]
          recb = [_sb(f"a_rb{i}", [128, 512], F32) for i in range(2)]
          ps_s = [_ps(f"a_ps{i}", [128, 512]) for i in range(3)]
          ps_o = [_ps(f"a_po{i}", [128, 512]) for i in range(3)]
          ps_d = [_ps(f"a_pd{i}", [128, 512]) for i in range(2)]
          w1q = make_w1_quanta(_sb) if upto >= 7 else []
          w1i = [0]

          def drain_w1(n):
              for _ in range(n):
                  if w1i[0] < len(w1q):
                      w1q[w1i[0]]()
                      w1i[0] += 1

          LA = 3
          tags = {}
          hi = 0
          gt = 0
          for qc in range(4):
              nst = NST[qc]
              wait("sync", TE, tags.get(("qcdone", qc - 1), 0))
              qt_tag = dma("sync", qt_sb[:],
                           QT_h[:, :, qc * 512:(qc + 1) * 512].rearrange("t p r -> p t r"), LD)
              if qc > 0:
                  wait("sync", VE, tags[("otn", hi - 1)])
              dm_tag = dma("sync", dm_sb[0][:], dmask[:, qc, :, :], LD)
              for kvh in range(KVH):
                  ki = qc * KVH + kvh
                  kp = ki % 2
                  wait("sync", TE, tags.get(("kidone", ki - 2), 0))
                  kl = dma("sync", kt_b[kp][:, 0:nst * 128], KT_h[kvh, :, 0:nst * 128], LD)
                  vl_ = dma("sync", v_b[kp][:, 0:nst, :],
                            V_h[kvh, 0:nst, :, :].rearrange("s p d -> p s d"), LD)
                  wait("tensor", LD, vl_)
                  if kvh == 0:
                      wait("tensor", LD, qt_tag)
                      wait("vector", LD, dm_tag)
                  for qh in range(4):
                      head = kvh * 4 + qh
                      bo = hi % 3
                      bd = hi % 2
                      wait("tensor", VE, tags.get(("otn", hi - 3), 0))
                      wait("tensor", SE, tags.get(("dcp", hi - 2), 0))
                      for st in range(nst + LA):
                          if st < nst:
                              bs = gt % 3
                              wait("tensor", SE, tags.get(("exp", gt - 3), 0))
                              stag = op("tensor", lambda e, bs=bs, st=st, head=head, kp=kp:
                                        e.matmul(ps_s[bs][:], kt_b[kp][:, st * 128:(st + 1) * 128],
                                                 qt_sb[:, head, :], start=True, stop=True,
                                                 skip_group_check=True), TE)
                              wait("scalar", TE, stag)
                              wait("scalar", TE, tags.get(("av", gt - NPT), 0))
                              etag = op("scalar", lambda e, bs=bs, bp=gt % NPT, st=st, qc=qc:
                                        e.activation(pt[bp][:], ps_s[bs][:], AF.Exp,
                                                     bias=bias_sb[:, qc, st:st + 1],
                                                     scale=SCALE), SE)
                              tags[("exp", gt)] = etag
                              if 8 * qc <= st < 8 * qc + 8:
                                  wait("vector", SE, etag)
                                  mtag = op("vector", lambda e, bp=gt % NPT, st=st, qc=qc:
                                            e.tensor_tensor(pt[bp][:], pt[bp][:],
                                                            dm_sb[0][:, st - 8 * qc, :],
                                                            ALU.mult), VE)
                                  tags[("msk", gt)] = mtag
                          if st >= LA:
                              dt_ = st - LA
                              dgt = gt - LA if st < nst else gt - (nst + LA - st)
                              if ("msk", dgt) in tags:
                                  wait("tensor", VE, tags[("msk", dgt)])
                              else:
                                  wait("tensor", SE, tags[("exp", dgt)])
                              op("tensor", lambda e, bd=bd, bp=dgt % NPT, dt_=dt_, nst=nst:
                                 e.matmul(ps_d[bd][0:1, :], ones[:], pt[bp][:],
                                          start=(dt_ == 0), stop=(dt_ == nst - 1),
                                          skip_group_check=True))
                              avt = op("tensor", lambda e, bo=bo, bp=dgt % NPT, dt_=dt_,
                                       nst=nst, kp=kp:
                                       e.matmul(ps_o[bo][:], v_b[kp][:, dt_, :], pt[bp][:],
                                                start=(dt_ == 0), stop=(dt_ == nst - 1),
                                                skip_group_check=True), TE)
                              tags[("av", dgt)] = avt
                          if st < nst:
                              gt += 1
                      hdone = tags[("av", gt - 1)]
                      wait("scalar", TE, hdone)
                      wait("scalar", VE, tags.get(("rec", hi - 2), 0))
                      dtag = op("scalar", lambda e, bd=bd, hp=hi % 2: e.copy(
                          den_sb[hp][:], ps_d[bd][0:1, :]), SE)
                      tags[("dcp", hi)] = dtag
                      wait("vector", SE, dtag)
                      wait("vector", ST, tags.get(("rst", hi - 2), 0))
                      rtag = op("vector", lambda e, hp=hi % 2: e.reciprocal(
                          rec_sb[hp][:], den_sb[hp][:]), VE)
                      tags[("rec", hi)] = rtag
                      wait("scalar", VE, rtag)
                      rst = dma("scalar", den_h[qc, head:head + 1, :], rec_sb[hi % 2][:], ST)
                      tags[("rst", hi)] = rst
                      wait("sync", ST, rst)
                      wait("sync", VE, tags.get(("otn", hi - 2), 0))
                      rbl = dma("sync", recb[hi % 2][:],
                                den_h[qc, head:head + 1, :].partition_broadcast(128), LD)
                      tags[("rbl", hi)] = rbl
                      wait("vector", LD, rbl)
                      wait("vector", TE, hdone)
                      wait("vector", ST, tags.get(("otstore", qc - 1), 0))
                      otn = op("vector", lambda e, bo=bo, head=head, hp=hi % 2: e.tensor_tensor(
                          ot_st[:, head, :], ps_o[bo][:], recb[hp][:], ALU.mult), VE)
                      tags[("otn", hi)] = otn
                      hi += 1
                      drain_w1(3 if upto >= 7 else 0)
                  tags[("kidone", ki)] = TE.n
              wait("scalar", VE, tags[("otn", hi - 1)])
              tags[("otstore", qc)] = dma(
                  "scalar", OT_h[:, :, qc * 512:(qc + 1) * 512].rearrange("t p r -> p t r"),
                  ot_st[:], ST)
              tags[("qcdone", qc)] = TE.n
          drain_w1(len(w1q))
          wait("sync", ST, tags[("otstore", 3)])
          barrier()
    _phase_atn()
    if upto < 7:
        return finish(nc, plan, ST)

    def _phase_oproj():
      with contextlib.ExitStack() as ph:
          _sb = lambda n, s, d: ph.enter_context(nc.sbuf_tensor(n, s, d))
          _ps = lambda n, s, d=F32: ph.enter_context(nc.psum_tensor(n, s, d))
          ot_sb = _sb("o_ot", [128, QH, RO], BF16)
          wo_sb = [_sb(f"o_w{i}", [128, HT, 128], BF16) for i in range(2)]
          yt_sb = [_sb(f"o_yt{i}", [128, 512], BF16) for i in range(2)]
          ps_y = [_ps(f"o_ps{i}", [128, 512]) for i in range(2)]
          lo = dma("sync", ot_sb[:], OT_h[:].rearrange("t p r -> p t r"), LD)
          wait("tensor", LD, lo)
          tags = {}
          g = 0
          for ho in range(HT):
              p = ho % 2
              wait("gpsimd", TE, tags.get(("ymm", ho - 2, 3), 0))
              gw = dma("gpsimd", wo_sb[p][:],
                       woT[:, ho * 128:(ho + 1) * 128].rearrange("(t p) d -> p t d", p=128), GP)
              wait("tensor", GP, gw)
              for rc in range(4):
                  bk = g % 2
                  wait("tensor", SE, tags.get(("yevac", g - 2), 0))

                  def ymm(e, p=p, rc=rc, bk=bk):
                      last = None
                      for dt in range(QH):
                          last = e.matmul(ps_y[bk][:], wo_sb[p][:, dt, :],
                                          ot_sb[:, dt, rc * 512:(rc + 1) * 512],
                                          start=(dt == 0), stop=(dt == QH - 1))
                      return last
                  tmm = op("tensor", ymm, TE)
                  tags[("ymm", ho, rc)] = tmm
                  wait("scalar", TE, tmm)
                  wait("scalar", ST, tags.get(("yst", g - 2), 0))
                  sev = op("scalar", lambda e, bk=bk: e.copy(yt_sb[bk][:], ps_y[bk][:]), SE)
                  tags[("yevac", g)] = sev
                  tags[("yst", g)] = dma("scalar", YT_h[ho, :, rc * 512:(rc + 1) * 512],
                                         yt_sb[bk][:], ST)
                  g += 1
          wait("sync", ST, tags[("yst", g - 1)])
          barrier()
    _phase_oproj()
    if upto < 8:
        return finish(nc, plan, ST)

    def _phase_o2():
      with contextlib.ExitStack() as ph:
          _sb = lambda n, s, d: ph.enter_context(nc.sbuf_tensor(n, s, d))
          _ps = lambda n, s, d=F32: ph.enter_context(nc.psum_tensor(n, s, d))
          ytc = [_sb(f"o2_yt{i}", [128, HT, 512], BF16) for i in range(2)]
          xb = [_sb(f"o2_x{i}", [128, D], F32) for i in range(2)]
          y1b = [_sb(f"o2_y1{i}", [128, D], F32) for i in range(2)]
          sq_scr = _sb("o2_sq", [128, D], F32)
          stat = [_sb(f"o2_st{i}", [128, 8], F32) for i in range(2)]
          lny = [_sb(f"o2_lny{i}", [128, D], BF16) for i in range(2)]
          lst = [_sb(f"o2_ls{i}", [128, HT, 128], BF16) for i in range(2)]
          ps_nat = _ps("o2_nat", [128, D], BF16)
          ps_tr = [_ps(f"o2_ptr{i}", [128, 512], BF16) for i in range(2)]
          tags = {}
          for rc in range(4):
              pc_ = rc % 2
              wait("sync", TE, tags.get(("natdone", rc - 2), 0))
              yl = dma("sync", ytc[pc_][:],
                       YT_h[:, :, rc * 512:(rc + 1) * 512].rearrange("t p r -> p t r"), LD)
              wait("tensor", LD, yl)
              for rsub in range(4):
                  i = rc * 4 + rsub
                  pi = i % 2
                  wait("sync", VE, tags.get(("y1", i - 2), 0))
                  lx = dma("sync", xb[pi][:], x_own[i * 128:(i + 1) * 128, :], LD)
                  wait("tensor", VE, tags.get(("y1", i - 1), 0))

                  def natf(e, pc_=pc_, rsub=rsub):
                      last = None
                      for ho in range(HT):
                          last = e.matmul(ps_nat[:, ho * 128:(ho + 1) * 128],
                                          ytc[pc_][:, ho, rsub * 128:(rsub + 1) * 128], ident[:],
                                          start=(ho % 8 == 0), stop=(ho % 8 == 7),
                                          is_transpose=True, skip_group_check=True)
                      return last
                  nt = op("tensor", natf, TE)
                  tags[("nat", i)] = nt
                  wait("vector", TE, nt)
                  wait("vector", LD, lx)
                  wait("vector", ST, tags.get(("y1st", i - 2), 0))
                  vy = op("vector", lambda e, pi=pi: e.tensor_tensor(
                      y1b[pi][:], ps_nat[:], xb[pi][:], ALU.add), VE)
                  tags[("y1", i)] = vy
                  wait("scalar", VE, vy)
                  tags[("y1st", i)] = dma("scalar", y1_h[i * 128:(i + 1) * 128, :], y1b[pi][:], ST)
                  wait("scalar", LD, lx)
                  vl = emit_ln_tile(y1b[pi], stat[pi], lny[pi], sq_scr, tags, 100 + i, 100 + i - 2)
                  wait("tensor", ST, tags.get(("lnyst", i - 2), 0))
                  emit_transpose_tile(lny[pi], lst[pi], 0, ps_tr, tags, 1000 + i * 8, vl)
                  lastev = tags[("trev", 1000 + i * 8 + 7)]
                  wait("scalar", SE, lastev)
                  tags[("lnyst", i)] = dma(
                      "scalar",
                      lnyT_h[:, :, i * 128:(i + 1) * 128].rearrange("t p r -> p t r"),
                      lst[pi][:], ST)
              tags[("natdone", rc)] = TE.n
          wait("sync", ST, tags[("lnyst", 15)])
          barrier()
    _phase_o2()
    if upto < 9:
        return finish(nc, plan, ST)

    def _phase_gate():
      with contextlib.ExitStack() as ph:
          _sb = lambda n, s, d: ph.enter_context(nc.sbuf_tensor(n, s, d))
          _ps = lambda n, s, d=F32: ph.enter_context(nc.psum_tensor(n, s, d))
          lnyT = _sb("g_lny", [128, HT, RO], BF16)
          wg_sb = [_sb(f"g_wg{i}", [128, HT, 128], BF16) for i in range(2)]
          wu_sb = [_sb(f"g_wu{i}", [128, HT, 128], BF16) for i in range(2)]
          s1_sb = [_sb(f"g_s1{i}", [128, 512], BF16) for i in range(2)]
          h_st = [_sb(f"g_h{i}", [128, RO], BF16) for i in range(2)]
          ps_g = [_ps(f"g_pg{i}", [128, 512]) for i in range(2)]
          ps_u = [_ps(f"g_pu{i}", [128, 512]) for i in range(2)]
          ll = dma("sync", lnyT[:], lnyT_h[:].rearrange("t p r -> p t r"), LD)
          wait("tensor", LD, ll)
          tags = {}
          for t in range(TD):
              p = t % 2
              wait("sync", TE, tags.get(("gdone", t - 2), 0))
              dma("sync", wg_sb[p][:],
                  Wgu_h[:, t * 128:(t + 1) * 128].rearrange("(h p) i -> p h i", p=128), LD)
              lw = dma("sync", wu_sb[p][:],
                  Wgu_h[:, (t + TD) * 128:(t + TD + 1) * 128].rearrange("(h p) i -> p h i", p=128), LD)
              wait("tensor", LD, lw)
              for rc in range(4):
                  g = t * 4 + rc
                  bk = g % 2
                  wait("tensor", VE, tags.get(("hm", g - 2), 0))

                  def gmm(e, p=p, rc=rc, bk=bk):
                      last = None
                      for ht in range(HT):
                          last = e.matmul(ps_g[bk][:], wg_sb[p][:, ht, :],
                                          lnyT[:, ht, rc * 512:(rc + 1) * 512],
                                          start=(ht == 0), stop=(ht == HT - 1))
                      for ht in range(HT):
                          last = e.matmul(ps_u[bk][:], wu_sb[p][:, ht, :],
                                          lnyT[:, ht, rc * 512:(rc + 1) * 512],
                                          start=(ht == 0), stop=(ht == HT - 1),
                                          skip_group_check=True)
                      return last
                  tmm = op("tensor", gmm, TE)
                  wait("scalar", TE, tmm)
                  wait("scalar", VE, tags.get(("hm", g - 2), 0))
                  stag = op("scalar", lambda e, bk=bk: e.activation(
                      s1_sb[bk][:], ps_g[bk][:], AF.Silu), SE)
                  wait("vector", SE, stag)
                  if rc == 0:
                      wait("vector", ST, tags.get(("hst", t - 2), 0))
                  hm = op("vector", lambda e, bk=bk, p=p, rc=rc: e.tensor_tensor(
                      h_st[p][:, rc * 512:(rc + 1) * 512], ps_u[bk][:], s1_sb[bk][:],
                      ALU.mult), VE)
                  tags[("hm", g)] = hm
              wait("scalar", VE, tags[("hm", t * 4 + 3)])
              tags[("hst", t)] = dma("scalar", H_h[t, :, :], h_st[p][:], ST)
              tags[("gdone", t)] = TE.n
          wait("sync", ST, tags[("hst", TD - 1)])
          barrier()
    _phase_gate()
    if upto < 10:
        return finish(nc, plan, ST)

    def _phase_down():
      with contextlib.ExitStack() as ph:
          _sb = lambda n, s, d: ph.enter_context(nc.sbuf_tensor(n, s, d))
          _ps = lambda n, s, d=F32: ph.enter_context(nc.psum_tensor(n, s, d))
          hc = _sb("d_hc", [128, TD, 512], BF16)
          wdh = [_sb(f"d_w{i}", [128, TD, 128], BF16) for i in range(2)]
          dstage = _sb("d_ds", [128, HT, 512], BF16)
          y1b = [_sb("d_y10", [128, D], F32)]
          ob = [_sb("d_o0", [128, D], F32)]
          ps_d = [_ps(f"d_pd{i}", [128, 512]) for i in range(2)]
          ps_nat = _ps("d_nat", [128, D], BF16)
          tags = {}
          for rc in range(4):
              wait("sync", TE, tags.get(("mmdone", rc - 1), 0))
              hl = dma("sync", hc[:], H_h[:, :, rc * 512:(rc + 1) * 512].rearrange("t p r -> p t r"), LD)
              for ho in range(HT):
                  p = ho % 2
                  g = rc * HT + ho
                  wait("sync", TE, tags.get(("dmm", g - 2), 0))
                  lw = dma("sync", wdh[p][:],
                           Wdn_h[:, :, ho * 128:(ho + 1) * 128].rearrange("t p h -> p t h"), LD)
                  wait("tensor", LD, lw)
                  if ho == 0:
                      wait("tensor", LD, hl)
                  wait("tensor", SE, tags.get(("devac", g - 2), 0))
                  if ho == 0:
                      wait("scalar", TE, tags.get(("natlast", rc - 1), 0))

                  def dmm(e, p=p, bk=g % 2):
                      last = None
                      for t in range(TD):
                          last = e.matmul(ps_d[bk][:], wdh[p][:, t, :], hc[:, t, :],
                                          start=(t == 0), stop=(t == TD - 1))
                      return last
                  tmm = op("tensor", dmm, TE)
                  tags[("dmm", g)] = tmm
                  wait("scalar", TE, tmm)
                  sev = op("scalar", lambda e, ho=ho, bk=g % 2: e.copy(
                      dstage[:, ho, :], ps_d[bk][:]), SE)
                  tags[("devac", g)] = sev
              tags[("devac-last", rc)] = tags[("devac", rc * HT + HT - 1)]
              tags[("mmdone", rc)] = TE.n
              wait("tensor", SE, tags[("devac-last", rc)])
              for rsub in range(4):
                  i = rc * 4 + rsub
                  pi = 0
                  wait("sync", VE, tags.get(("y2", i - 1), 0))
                  ly = dma("sync", y1b[pi][:], y1_h[i * 128:(i + 1) * 128, :], LD)
                  wait("tensor", VE, tags.get(("y2", i - 1), 0))

                  def natf(e, rsub=rsub):
                      last = None
                      for ho in range(HT):
                          last = e.matmul(ps_nat[:, ho * 128:(ho + 1) * 128],
                                          dstage[:, ho, rsub * 128:(rsub + 1) * 128], ident[:],
                                          start=(ho % 8 == 0), stop=(ho % 8 == 7),
                                          is_transpose=True, skip_group_check=True)
                      return last
                  nt = op("tensor", natf, TE)
                  if rsub == 3:
                      tags[("natlast", rc)] = nt
                  wait("vector", TE, nt)
                  wait("vector", LD, ly)
                  wait("vector", ST, tags.get(("ost", i - 1), 0))
                  vy = op("vector", lambda e, pi=pi: e.tensor_tensor(
                      ob[pi][:], ps_nat[:], y1b[pi][:], ALU.add), VE)
                  tags[("y2", i)] = vy
                  wait("scalar", VE, vy)
                  tags[("ost", i)] = dma("scalar", out[i * 128:(i + 1) * 128, :], ob[pi][:], ST)
          wait("sync", ST, tags[("ost", 15)])
          barrier()

    _phase_down()
    return finish(nc, plan, ST)


def finish(nc, plan, ST):
    with nc.Block() as block:
        @block.sync
        def _(eng):
            for fn in plan.ops["sync"]:
                fn(eng)

        @block.scalar
        def _(eng):
            for fn in plan.ops["scalar"]:
                fn(eng)

        @block.vector
        def _(eng):
            for fn in plan.ops["vector"]:
                fn(eng)

        @block.tensor
        def _(eng):
            for fn in plan.ops["tensor"]:
                fn(eng)

        @block.gpsimd
        def _(eng):
            for fn in plan.ops["gpsimd"]:
                fn(eng)
    return nc



def _rope_tables():
    inv = 1.0 / (10000.0 ** (np.arange(0, 128, 2, dtype=np.float64) / 128.0))
    pos = np.arange(S, dtype=np.float64)
    fr = pos[:, None] * inv[None, :]
    emb = np.concatenate([fr, fr], axis=1)
    cos = np.cos(emb).astype(np.float32).T.copy()
    sin = np.sin(emb).astype(np.float32).T.copy()
    sin_hat = sin.copy()
    sin_hat[0:64] *= -1.0
    return cos, sin_hat


def _core_inputs(inputs):
    x = np.asarray(inputs["L0_attn_hidden_states"], np.float32)
    wq = np.asarray(inputs["L0_attn_Wq"], np.float32)
    wk = np.asarray(inputs["L0_attn_Wk"], np.float32)
    wv = np.asarray(inputs["L0_attn_Wv"], np.float32)
    wo = np.asarray(inputs["L0_attn_Wo"], np.float32)
    qwg = np.asarray(inputs["L0_ffn_qweight_gu"], np.int32)
    qzg = np.asarray(inputs["L0_ffn_qzeros_gu"], np.int32)
    scg = np.asarray(inputs["L0_ffn_scales_gu"], np.float32)
    qwd = np.asarray(inputs["L0_ffn_qweight_dn"], np.int32)
    qzd = np.asarray(inputs["L0_ffn_qzeros_dn"], np.int32)
    scd = np.asarray(inputs["L0_ffn_scales_dn"], np.float32)

    shared = {
        "wqT": np.ascontiguousarray(wq.T[PH_PERM]),
        "wkT": np.ascontiguousarray(wk.T[PH_PERM]),
        "wvT": np.ascontiguousarray(wv.T[PH_PERM]),
        "woT": np.ascontiguousarray(wo.T[:, PH_PERM]),
        "qw_gu": np.ascontiguousarray(qwg[PH_PERM]),
        "qz_gu": qzg,
        "sc_gu": np.ascontiguousarray(scg[:, PI_PERM]),
        "qw_dn": np.ascontiguousarray(qwd[PI_DN]),
        "qz_dn": qzd,
        "sc_dn": np.ascontiguousarray(scd[:, PH_PERM]),
    }
    cos, sin_hat = _rope_tables()
    shared["cos_k"] = cos
    shared["sin_k"] = sin_hat

    maps = []
    for c in range(8):
        b, half = c // 2, c % 2
        chunks = CHUNKS[half]
        xb = np.ascontiguousarray(x[b][:, PH_PERM])
        rows = np.concatenate([np.arange(ch * 512, (ch + 1) * 512) for ch in chunks])
        m = dict(shared)
        m["x_kv"] = xb
        m["x_own"] = np.ascontiguousarray(xb[rows])
        m["cos_q"] = np.ascontiguousarray(cos[:, rows])
        m["sin_q"] = np.ascontiguousarray(sin_hat[:, rows])
        bias = np.zeros((128, 4, 32), np.float32)
        for j, ch in enumerate(chunks):
            nst_true = (ch + 1) * 4
            bias[:, j, nst_true:] = NEG
        m["biascfg"] = bias
        dm = np.zeros((128, 4, 8, 512), np.float32)
        p = np.arange(128)
        r = np.arange(512)
        for j, ch in enumerate(chunks):
            for site in range(8):
                st = 8 * j + site
                sglob = st * 128 + p
                rglob = ch * 512 + r
                dm[:, j, site, :] = (sglob[:, None] <= rglob[None, :]).astype(np.float32)
        m["dmask"] = dm.astype(ml_dtypes.bfloat16)
        maps.append(m)
    return maps


_prog_cache = {}


def kernel(**inputs):
    maps = _core_inputs(inputs)
    key = (DBG_FLAGS["dbg"], DBG_FLAGS["upto"])
    if key not in _prog_cache:
        _prog_cache[key] = build_program(dbg=key[0], upto=key[1])
    nc = _prog_cache[key]
    res = run_bass_kernel_spmd(nc, maps, core_ids=list(range(8)))
    if DBG_FLAGS["dbg"]:
        kernel.last_results = res.results
    y = np.empty((4, S, D), np.float32)
    for c in range(8):
        b, half = c // 2, c % 2
        o = np.asarray(res.results[c]["out"])
        for j, ch in enumerate(CHUNKS[half]):
            blk = np.empty((512, D), np.float32)
            blk[:, PH_PERM] = o[j * 512:(j + 1) * 512]
            y[b, ch * 512:(ch + 1) * 512] = blk
    return y


DBG_FLAGS = {"dbg": False, "upto": 99}


# version 05
# speedup vs baseline: 1.2635x; 1.0585x over previous
import contextlib
import numpy as np
import ml_dtypes
import concourse.bass as bass
import concourse.mybir as mybir
from concourse.bass_utils import run_bass_kernel_spmd

F32 = mybir.dt.float32
BF16 = mybir.dt.bfloat16
I32 = mybir.dt.int32
AF = mybir.ActivationFunctionType
ALU = mybir.AluOpType

D = 4096
S = 4096
RO = 2048
HT = 32
QH = 32
KVH = 8
INTER = 11008
I2 = 22016
PC = 2752
TG = 172
TD = 86
SBOUND = [1024, 2048, 3072, 4096]
NST = [8, 16, 24, 32]
CHUNKS = [[0, 3, 4, 7], [1, 2, 5, 6]]
EPS = 1e-5
SCALE = float(1.0 / np.sqrt(128.0))
NEG = -60000.0

ENGS = ("sync", "scalar", "vector", "tensor", "gpsimd")
CFG = {"w1_inline": False, "la": 3}


def _perm_block(n):
    idx = np.arange(n)
    t, rem = idx // 128, idx % 128
    j, v = rem // 16, rem % 16
    return 128 * t + 8 * v + j


PH_PERM = _perm_block(D)
PI_PERM = _perm_block(I2)
PI_DN = _perm_block(INTER)


class Sem:
    def __init__(self, nc, es, name):
        self.h = es.enter_context(nc.semaphore(name))
        self.n = 0

    def add(self, k):
        self.n += k
        return self.n


class Plan:
    def __init__(self):
        self.ops = {e: [] for e in ENGS}

    def emit(self, eng, fn):
        self.ops[eng].append(fn)


def build_program(dbg=False, upto=99):
    nc = bass.Bass("TRN2", target_bir_lowering=False)
    es = contextlib.ExitStack()
    plan = Plan()
    skind = "ExternalOutput" if dbg else "Internal"

    def din(name, shape, dt):
        return nc.dram_tensor(name, shape, dt, kind="ExternalInput").ap()

    x_kv = din("x_kv", [S, D], F32)
    x_own = din("x_own", [RO, D], F32)
    wqT = din("wqT", [D, D], F32)
    wkT = din("wkT", [D, 1024], F32)
    wvT = din("wvT", [D, 1024], F32)
    woT = din("woT", [D, D], F32)
    cos_k = din("cos_k", [128, S], F32)
    sin_k = din("sin_k", [128, S], F32)
    cos_q = din("cos_q", [128, RO], F32)
    sin_q = din("sin_q", [128, RO], F32)
    qw_gu = din("qw_gu", [D, PC], I32)
    qz_gu = din("qz_gu", [32, PC], I32)
    sc_gu = din("sc_gu", [32, I2], F32)
    qw_dn = din("qw_dn", [INTER, 512], I32)
    qz_dn = din("qz_dn", [TD, 512], I32)
    sc_dn = din("sc_dn", [TD, D], F32)
    dmask = din("dmask", [128, 4, 8, 512], BF16)
    biascfg = din("biascfg", [128, 4, 32], F32)

    out = nc.dram_tensor("out", [RO, D], F32, kind="ExternalOutput").ap()

    def scr(name, shape, dt):
        return nc.dram_tensor(name, shape, dt, kind=skind).ap()

    zrow_gu = scr("zrow_gu", [32, I2], BF16)
    srow_gu = scr("srow_gu", [32, I2], BF16)
    zrow_dn = scr("zrow_dn", [TD, D], BF16)
    srow_dn = scr("srow_dn", [TD, D], BF16)
    Wgu_h = scr("Wgu_h", [D, I2], BF16)
    Wdn_h = scr("Wdn_h", [TD, 128, D], BF16)
    LNxT_h = scr("LNxT_h", [HT, 128, S], BF16)
    LNoT_h = scr("LNoT_h", [HT, 128, RO], BF16)
    KT_h = scr("KT_h", [KVH, 128, S], BF16)
    V_h = scr("V_h", [KVH, 32, 128, 128], BF16)
    QT_h = scr("QT_h", [QH, 128, RO], BF16)
    OT_h = scr("OT_h", [QH, 128, RO], BF16)
    YT_h = scr("YT_h", [HT, 128, RO], BF16)
    y1_h = scr("y1_h", [RO, D], F32)
    lnyT_h = scr("lnyT_h", [HT, 128, RO], BF16)
    H_h = scr("H_h", [TD, 128, RO], BF16)
    den_h = scr("den_h", [4, QH, 512], F32)

    ident_t = nc.inline_tensor(np.eye(128, dtype=ml_dtypes.bfloat16), name="identb")
    ones_t = nc.inline_tensor(np.ones((128, 1), dtype=ml_dtypes.bfloat16), name="onesb")

    LD = Sem(nc, es, "LD")
    ST = Sem(nc, es, "ST")
    GP = Sem(nc, es, "GP")
    TE = Sem(nc, es, "TE")
    VE = Sem(nc, es, "VE")
    SE = Sem(nc, es, "SE")
    PB = Sem(nc, es, "PB")

    phase_no = [0]

    def barrier():
        target = 5 * (phase_no[0] + 1)
        for e in ENGS:
            plan.emit(e, lambda eng: eng.sem_inc(PB.h, 1))
        for e in ENGS:
            plan.emit(e, lambda eng, t=target: eng.wait_ge(PB.h, t))
        phase_no[0] += 1
        PB.n = target

    def dma(queue, o, i, sem):
        def fn(eng, o=o, i=i, s=sem.h):
            eng.dma_start(out=o, in_=i).then_inc(s, 16)
        plan.emit(queue, fn)
        return sem.add(16)

    def wait(eng, sem, tag):
        if tag and tag > 0:
            plan.emit(eng, lambda e, s=sem.h, t=tag: e.wait_ge(s, t))

    def op(eng, fn, sem=None):
        if sem is None:
            plan.emit(eng, fn)
            return None
        tag = sem.add(1)

        def wrapped(e, fn=fn, s=sem.h):
            fn(e).then_inc(s, 1)
        plan.emit(eng, wrapped)
        return tag

    def gsb(name, shape, dt):
        return es.enter_context(nc.sbuf_tensor(name, shape, dt))

    ident = gsb("g_ident", [128, 128], BF16)
    ones = gsb("g_ones", [128, 1], BF16)
    bias_sb = gsb("g_bias", [128, 4, 32], F32)
    eps_sb = gsb("g_eps", [128, 1], F32)
    flush_v = gsb("g_flv", [128, 384], F32)
    plan.emit("vector", lambda e: e.memset(eps_sb[:], EPS))
    dma("sync", ident[:], ident_t.ap(), LD)
    dma("sync", ones[:], ones_t.ap(), LD)
    gtag = dma("sync", bias_sb[:], biascfg[:], LD)
    for e in ("vector", "tensor", "scalar", "gpsimd"):
        wait(e, LD, gtag)

    def vfence(tag):
        wait("vector", VE, tag)

    def emit_ln_tile(xb, stat, lnx, sq_scr, tags, key, prev_key):
        t1 = op("vector", lambda e: e.reduce_sum(stat[:, 0:1], xb[:],
                                                 axis=mybir.AxisListType.X), VE)
        wait("scalar", VE, tags.get(("vvfree", prev_key), 0))
        stg = op("scalar", lambda e: e.activation(sq_scr[:], xb[:], AF.Square,
                                                  accum_out=stat[:, 1:2]), SE)
        wait("vector", SE, stg)
        vfence(t1)
        t2 = op("vector", lambda e: e.tensor_scalar(stat[:, 2:3], stat[:, 0:1], 1.0 / D,
                                                    None, ALU.mult), VE)
        t3 = op("vector", lambda e: e.tensor_scalar(stat[:, 3:4], stat[:, 1:2], 1.0 / D,
                                                    None, ALU.mult), VE)
        vfence(t2)
        t4 = op("vector", lambda e: e.tensor_tensor(stat[:, 4:5], stat[:, 2:3], stat[:, 2:3],
                                                    ALU.mult), VE)
        vfence(t4)
        vv = op("vector", lambda e: e.tensor_tensor(stat[:, 5:6], stat[:, 3:4], stat[:, 4:5],
                                                    ALU.subtract), VE)
        tags[("vvfree", key)] = vv
        wait("scalar", VE, vv)
        sst = op("scalar", lambda e: e.activation(stat[:, 6:7], stat[:, 5:6], AF.Sqrt,
                                                  bias=eps_sb[:]), SE)
        wait("vector", SE, sst)
        t5 = op("vector", lambda e: e.reciprocal(stat[:, 7:8], stat[:, 6:7]), VE)
        vfence(t5)
        vl = op("vector", lambda e: e.tensor_scalar(lnx[:], xb[:], stat[:, 2:3], stat[:, 7:8],
                                                    ALU.subtract, ALU.mult), VE)
        tags[("lnx", key)] = vl
        return vl

    def emit_transpose_tile(lnx, dst, rcol, ps_tr, tags, gidx_base, vl_tag):
        wait("tensor", VE, vl_tag)
        last = 0
        for g in range(8):
            bk = (gidx_base + g) % 2
            wait("tensor", SE, tags.get(("trev", gidx_base + g - 2), 0))

            def trf(e, g=g, bk=bk, lnx=lnx):
                lastmm = None
                for u in range(4):
                    ht = g * 4 + u
                    lastmm = e.matmul(ps_tr[bk][:, u * 128:(u + 1) * 128],
                                      lnx[:, ht * 128:(ht + 1) * 128], ident[:],
                                      start=(u == 0), stop=(u == 3), is_transpose=True)
                return lastmm
            tt = op("tensor", trf, TE)
            wait("scalar", TE, tt)
            sev = op("scalar", lambda e, g=g, bk=bk, dst=dst, rcol=rcol: e.copy(
                dst[:, g * 4:(g + 1) * 4, rcol:rcol + 128],
                ps_tr[bk][:].rearrange("p (t r) -> p t r", r=128)), SE)
            tags[("trev", gidx_base + g)] = sev
            last = sev
        return last

    def emit_rope(psrc, cosb, sinb, coloff, rbufs, dst, tags=None):
        rot, t1, t2 = rbufs
        op("vector", lambda e: e.tensor_copy(rot[0:64, :], psrc[64:128, :]))
        op("vector", lambda e: e.tensor_copy(rot[64:128, :], psrc[0:64, :]))
        op("vector", lambda e: e.tensor_tensor(t1[:], psrc[:], cosb[:, coloff:coloff + 512],
                                               ALU.mult))
        op("vector", lambda e: e.tensor_tensor(t2[:], rot[:], sinb[:, coloff:coloff + 512],
                                               ALU.mult))
        return op("vector", lambda e: e.tensor_tensor(dst[:], t1[:], t2[:], ALU.add), VE)

    def _phase_w0():
      with contextlib.ExitStack() as ph:
          _sb = lambda n, s, d: ph.enter_context(nc.sbuf_tensor(n, s, d))
          with contextlib.ExitStack() as ph2:
              _sb2 = lambda n, s, d: ph2.enter_context(nc.sbuf_tensor(n, s, d))
              zq = _sb2("w0_zq", [32, PC], I32)
              zsm = _sb2("w0_zsm", [32, PC], I32)
              zb = _sb2("w0_zb", [32, I2], BF16)
              scf = _sb2("w0_scf", [32, INTER], F32)
              scb = _sb2("w0_scb", [32, I2], BF16)
              t1 = dma("sync", zq[:], qz_gu[:], LD)
              t2a = dma("sync", scf[:], sc_gu[:, 0:INTER], LD)
              wait("vector", LD, t2a)
              for j in range(8):
                  op("vector", lambda e, j=j: e.tensor_scalar(
                      zsm[:], zq[:], 4 * j, 15, ALU.logical_shift_right, ALU.bitwise_and))
                  op("vector", lambda e, j=j: e.tensor_copy(
                      zb[:].rearrange("g (t b v) -> g t b v", t=TG, b=8, v=16)[:, :, j, :],
                      zsm[:].rearrange("g (t v) -> g t v", v=16)))
              v1 = op("vector", lambda e: e.tensor_copy(scb[:, 0:INTER], scf[:]), VE)
              wait("sync", VE, v1)
              t2b = dma("sync", scf[:], sc_gu[:, INTER:I2], LD)
              wait("vector", LD, t2b)
              v2 = op("vector", lambda e: e.tensor_copy(scb[:, INTER:I2], scf[:]), VE)
              wait("scalar", VE, v2)
              dma("scalar", zrow_gu[:], zb[:], ST)
              stg = dma("scalar", srow_gu[:], scb[:], ST)
              wait("sync", ST, stg)
          zq2 = _sb("w0_zq2", [TD, 512], I32)
          zsm2 = _sb("w0_zsm2", [TD, 512], I32)
          zb2 = _sb("w0_zb2", [TD, D], BF16)
          scf2 = _sb("w0_scf2", [TD, D], F32)
          scb2 = _sb("w0_scb2", [TD, D], BF16)
          t3 = dma("sync", zq2[:], qz_dn[:], LD)
          t4 = dma("sync", scf2[:], sc_dn[:], LD)
          wait("vector", LD, t4)
          for j in range(8):
              op("vector", lambda e, j=j: e.tensor_scalar(
                  zsm2[:], zq2[:], 4 * j, 15, ALU.logical_shift_right, ALU.bitwise_and))
              op("vector", lambda e, j=j: e.tensor_copy(
                  zb2[:].rearrange("g (T b v) -> g T b v", T=HT, b=8, v=16)[:, :, j, :],
                  zsm2[:].rearrange("g (T v) -> g T v", v=16)))
          vt = op("vector", lambda e: e.tensor_copy(scb2[:], scf2[:]), VE)
          wait("scalar", VE, vt)
          dma("scalar", zrow_dn[:], zb2[:], ST)
          stg = dma("scalar", srow_dn[:], scb2[:], ST)
          wait("sync", ST, stg)
          barrier()
    _phase_w0()
    if upto < 1:
        return finish(nc, plan, ST)

    def make_w1_quanta(_sb, qa="gpsimd", qb="sync", qc_="scalar"):
        NQ = 4
        QP = PC // NQ
        QI = QP * 8
        WW = QI
        qwb = [_sb(f"w1_qw{i}", [128, QP], I32) for i in range(2)]
        wint = _sb("w1_wint", [128, 352 * 8], I32)
        wd = [_sb(f"w1_wd{i}", [128, WW], BF16) for i in range(2)]
        zbc = [_sb(f"w1_zb{i}", [128, WW], BF16) for i in range(2)]
        sbc = [_sb(f"w1_sb{i}", [128, WW], BF16) for i in range(2)]
        tags = {}

        def w1_quant(ltag, g2s, p, npc, n_i, store_fn, it):
            wait("vector", GP, ltag)
            wait("vector", LD, g2s)
            wait("vector", ST, tags.get(("st", it - 2), 0))
            splits = [(0, 352), (352, npc - 352)] if npc > 352 else [(0, npc)]
            for off, w in splits:
                nb = w * 8 // 128
                ut = 0
                for j in range(8):
                    ut = op("vector", lambda e, j=j, p=p, nb=nb, off=off, w=w: e.tensor_scalar(
                        wint[:, 0:nb * 128].rearrange("p (t b v) -> p t b v", t=nb, b=8, v=16)[:, :, j, :],
                        qwb[p][:, off:off + w].rearrange("p (t v) -> p t v", v=16),
                        4 * j, 15, ALU.logical_shift_right, ALU.bitwise_and), VE if j == 7 else None)
                wait("vector", VE, ut)
                op("vector", lambda e, p=p, off=off, nb=nb: e.tensor_copy(
                    wd[p][:, off * 8:off * 8 + nb * 128], wint[:, 0:nb * 128]))
            op("vector", lambda e, p=p, n=n_i: e.tensor_tensor(
                wd[p][:, 0:n], wd[p][:, 0:n], zbc[p][:, 0:n], ALU.subtract))
            vt = op("vector", lambda e, p=p, n=n_i: e.tensor_tensor(
                wd[p][:, 0:n], wd[p][:, 0:n], sbc[p][:, 0:n], ALU.mult), VE)
            tags[("vfree", it)] = vt
            wait("scalar", VE, vt)
            tags[("st", it)] = store_fn()

        quanta = []

        def mk_gu(ht, q, it):
            def emit():
                p = it % 2
                wait(qa, VE, tags.get(("vfree", it - 2), 0))
                dma(qa, qwb[p][:, 0:QP],
                    qw_gu[ht * 128:(ht + 1) * 128, q * QP:(q + 1) * QP], GP)
                ltag = dma(qa, zbc[p][:, 0:QI],
                    zrow_gu[ht:ht + 1, q * QI:(q + 1) * QI].partition_broadcast(128), GP)
                wait(qb, VE, tags.get(("vfree", it - 2), 0))
                g2s = dma(qb, sbc[p][:, 0:QI],
                          srow_gu[ht:ht + 1, q * QI:(q + 1) * QI].partition_broadcast(128), LD)
                def store(p=p, ht=ht, q=q):
                    return dma(qc_,
                               Wgu_h[ht * 128:(ht + 1) * 128, q * QI:(q + 1) * QI],
                               wd[p][:, 0:QI], ST)
                w1_quant(ltag, g2s, p, QP, QI, store, it)
            return emit

        def mk_dn(t, it):
            def emit():
                p = it % 2
                wait(qa, VE, tags.get(("vfree", it - 2), 0))
                dma(qa, qwb[p][:, 0:512], qw_dn[t * 128:(t + 1) * 128, :], GP)
                ltag = dma(qa, zbc[p][:, 0:D], zrow_dn[t:t + 1, :].partition_broadcast(128), GP)
                wait(qb, VE, tags.get(("vfree", it - 2), 0))
                g2s = dma(qb, sbc[p][:, 0:D], srow_dn[t:t + 1, :].partition_broadcast(128), LD)
                def store(p=p, t=t):
                    return dma(qc_, Wdn_h[t, :, :], wd[p][:, 0:D], ST)
                w1_quant(ltag, g2s, p, 512, D, store, it)
            return emit

        it = 0
        for ht in range(HT):
            for q in range(4):
                quanta.append(mk_gu(ht, q, it))
                it += 1
        for t in range(TD):
            quanta.append(mk_dn(t, it))
            it += 1
        return quanta

    def _phase_w1_standalone():
      with contextlib.ExitStack() as ph:
          _sb = lambda n, s, d: ph.enter_context(nc.sbuf_tensor(n, s, d))
          for fn in make_w1_quanta(_sb):
              fn()
          wait("sync", ST, ST.n)
          wait("sync", GP, GP.n)
          wait("sync", LD, LD.n)
          barrier()
    if upto < 7 or not CFG["w1_inline"]:
        _phase_w1_standalone()
    if upto < 2:
        return finish(nc, plan, ST)

    def emit_kv_phase(which):
        with contextlib.ExitStack() as ph:
            _sb = lambda n, s, d: ph.enter_context(nc.sbuf_tensor(n, s, d))
            _ps = lambda n, s, d=F32: ph.enter_context(nc.psum_tensor(n, s, d))
            w_sb = _sb(f"{which}_w", [128, HT, 1024], BF16)
            gw = dma("gpsimd", w_sb[:],
                     (wkT if which == "k" else wvT)[:].rearrange("(t p) d -> p t d", p=128), GP)
            ps_mm = [_ps(f"{which}_pm{i}", [128, 512]) for i in range(2)]
            tags = {}
            if which == "k":
                lnxT = _sb("k_lnxT", [128, HT, 512], BF16)
                xb = [_sb(f"k_x{i}", [128, D], F32) for i in range(2)]
                sq_scr = _sb("k_sq", [128, D], F32)
                stat = [_sb(f"k_st{i}", [128, 8], F32) for i in range(2)]
                lnx = [_sb(f"k_lnx{i}", [128, D], BF16) for i in range(2)]
                ps_tr = [_ps(f"k_ptr{i}", [128, 512], BF16) for i in range(2)]
                cosb = [_sb(f"k_cos{i}", [128, 512], F32) for i in range(2)]
                sinb = [_sb(f"k_sin{i}", [128, 512], F32) for i in range(2)]
                rope_b = [_sb(f"k_r{u}", [128, 512], F32) for u in range(3)]
                kt_sb = [_sb(f"k_kt{i}", [128, 512], BF16) for i in range(2)]
            else:
                lnxTb = [_sb(f"v_lnxT{i}", [128, HT, 512], BF16) for i in range(2)]
                v_sb = [_sb(f"v_o{i}", [128, 512], BF16) for i in range(2)]
            wait("tensor", GP, gw)
            for sc in range(8):
                if which == "k":
                    pcs = sc % 2
                    wait("sync", VE, tags.get(("rope", (sc - 2) * 8 + 7), 0))
                    dma("sync", cosb[pcs][:], cos_k[:, sc * 512:(sc + 1) * 512], LD)
                    cstag = dma("sync", sinb[pcs][:], sin_k[:, sc * 512:(sc + 1) * 512], LD)
                    tags[("cosld", sc)] = cstag
                    for rt in range(4):
                        i = sc * 4 + rt
                        pi = i % 2
                        wait("sync", VE, tags.get(("lnx", i - 2), 0))
                        lx = dma("sync", xb[pi][:], x_kv[i * 128:(i + 1) * 128, :], LD)
                        wait("vector", LD, lx)
                        wait("scalar", LD, lx)
                        vl = emit_ln_tile(xb[pi], stat[pi], lnx[pi], sq_scr, tags, i, i - 2)
                        if sc > 0 and rt == 0:
                            wait("tensor", ST, tags[("lnspill", sc - 1)])
                            wait("scalar", TE, tags[("kmms", sc - 1)])
                        emit_transpose_tile(lnx[pi], lnxT, rt * 128, ps_tr, tags, i * 8, vl)
                    lastev = tags[("trev", (sc * 4 + 3) * 8 + 7)]
                    wait("scalar", SE, lastev)
                    tags[("lnspill", sc)] = dma(
                        "scalar", LNxT_h[:, :, sc * 512:(sc + 1) * 512].rearrange("t p s -> p t s"),
                        lnxT[:], ST)
                    wait("tensor", SE, lastev)
                    for dt in range(8):
                        kg = sc * 8 + dt
                        bk = kg % 2
                        wait("tensor", VE, tags.get(("rope", kg - 2), 0))

                        def kmm(e, dt=dt, bk=bk):
                            last = None
                            for ht in range(HT):
                                last = e.matmul(ps_mm[bk][:], w_sb[:, ht, dt * 128:(dt + 1) * 128],
                                                lnxT[:, ht, :], start=(ht == 0), stop=(ht == HT - 1))
                            return last
                        tmm = op("tensor", kmm, TE)
                        wait("vector", TE, tmm)
                        wait("vector", ST, tags.get(("ktst", kg - 2), 0))
                        if dt == 0:
                            wait("vector", LD, tags[("cosld", sc)])
                        rtag = emit_rope(ps_mm[bk], cosb[pcs], sinb[pcs], 0, rope_b, kt_sb[bk])
                        tags[("rope", kg)] = rtag
                        wait("scalar", VE, rtag)
                        tags[("ktst", kg)] = dma(
                            "scalar", KT_h[dt, :, sc * 512:(sc + 1) * 512], kt_sb[bk][:], ST)
                    tags[("kmms", sc)] = TE.n
                else:
                    pc_ = sc % 2
                    wait("sync", TE, tags.get(("vmms", sc - 2), 0))
                    lt = dma("sync", lnxTb[pc_][:],
                             LNxT_h[:, :, sc * 512:(sc + 1) * 512].rearrange("t p s -> p t s"), LD)
                    wait("tensor", LD, lt)
                    for sl in range(4):
                        for dc in range(2):
                            g = sc * 8 + sl * 2 + dc
                            bk = g % 2
                            wait("tensor", SE, tags.get(("vevac", g - 2), 0))

                            def vmm(e, sl=sl, dc=dc, bk=bk, pc_=pc_):
                                last = None
                                for ht in range(HT):
                                    last = e.matmul(ps_mm[bk][:],
                                                    lnxTb[pc_][:, ht, sl * 128:(sl + 1) * 128],
                                                    w_sb[:, ht, dc * 512:(dc + 1) * 512],
                                                    start=(ht == 0), stop=(ht == HT - 1))
                                return last
                            tmm = op("tensor", vmm, TE)
                            wait("scalar", TE, tmm)
                            wait("scalar", ST, tags.get(("vst", g - 2), 0))
                            sev = op("scalar", lambda e, bk=bk: e.copy(v_sb[bk][:], ps_mm[bk][:]), SE)
                            tags[("vevac", g)] = sev
                            tags[("vst", g)] = dma(
                                "scalar",
                                V_h[dc * 4:(dc + 1) * 4, sc * 4 + sl, :, :].rearrange("k p d -> p k d"),
                                v_sb[bk][:].rearrange("p (k d) -> p k d", d=128), ST)
                    tags[("vmms", sc)] = TE.n
            last_st = max(v for k, v in tags.items() if k[0] in ("ktst", "vst", "lnspill"))
            wait("sync", ST, last_st)
        barrier()

    emit_kv_phase("k")
    if upto < 3:
        return finish(nc, plan, ST)
    emit_kv_phase("v")
    if upto < 4:
        return finish(nc, plan, ST)

    def _phase_q1():
      with contextlib.ExitStack() as ph:
          _sb = lambda n, s, d: ph.enter_context(nc.sbuf_tensor(n, s, d))
          _ps = lambda n, s, d=F32: ph.enter_context(nc.psum_tensor(n, s, d))
          lnoT = _sb("q1_lnoT", [128, HT, 512], BF16)
          xb = [_sb(f"q1_x{i}", [128, D], F32) for i in range(2)]
          sq_scr = _sb("q1_sq", [128, D], F32)
          stat = [_sb(f"q1_st{i}", [128, 8], F32) for i in range(2)]
          lnx = [_sb(f"q1_lnx{i}", [128, D], BF16) for i in range(2)]
          ps_tr = [_ps(f"q1_ptr{i}", [128, 512], BF16) for i in range(2)]
          tags = {}
          for sc in range(4):
              for rt in range(4):
                  i = sc * 4 + rt
                  pi = i % 2
                  wait("sync", VE, tags.get(("lnx", i - 2), 0))
                  lx = dma("sync", xb[pi][:], x_own[i * 128:(i + 1) * 128, :], LD)
                  wait("vector", LD, lx)
                  wait("scalar", LD, lx)
                  vl = emit_ln_tile(xb[pi], stat[pi], lnx[pi], sq_scr, tags, i, i - 2)
                  if sc > 0 and rt == 0:
                      wait("tensor", ST, tags[("spill", sc - 1)])
                  emit_transpose_tile(lnx[pi], lnoT, rt * 128, ps_tr, tags, i * 8, vl)
              lastev = tags[("trev", (sc * 4 + 3) * 8 + 7)]
              wait("scalar", SE, lastev)
              tags[("spill", sc)] = dma(
                  "scalar", LNoT_h[:, :, sc * 512:(sc + 1) * 512].rearrange("t p s -> p t s"),
                  lnoT[:], ST)
          wait("sync", ST, tags[("spill", 3)])
          barrier()
    _phase_q1()
    if upto < 5:
        return finish(nc, plan, ST)

    def _phase_q2():
      with contextlib.ExitStack() as ph:
          _sb = lambda n, s, d: ph.enter_context(nc.sbuf_tensor(n, s, d))
          _ps = lambda n, s, d=F32: ph.enter_context(nc.psum_tensor(n, s, d))
          lnq = _sb("q2_lnq", [128, HT, RO], BF16)
          wq_sb = [_sb(f"q2_w{i}", [128, HT, 128], BF16) for i in range(2)]
          cosb = _sb("q2_cos", [128, RO], F32)
          sinb = _sb("q2_sin", [128, RO], F32)
          rope_b = [_sb(f"q2_r{u}", [128, 512], F32) for u in range(3)]
          qt_st = [_sb(f"q2_qt{i}", [128, RO], BF16) for i in range(2)]
          ps_q = [_ps(f"q2_ps{i}", [128, 512]) for i in range(8)]
          dma("sync", cosb[:], cos_q[:], LD)
          dma("sync", sinb[:], sin_q[:], LD)
          lq = dma("sync", lnq[:], LNoT_h[:].rearrange("t p s -> p t s"), LD)
          wait("tensor", LD, lq)
          tags = {}
          for dt in range(QH):
              p = dt % 2
              wait("gpsimd", TE, tags.get(("qmm", dt - 2), 0))
              gw = dma("gpsimd", wq_sb[p][:],
                       wqT[:, dt * 128:(dt + 1) * 128].rearrange("(t p) d -> p t d", p=128), GP)
              wait("tensor", GP, gw)
              wait("tensor", VE, tags.get(("rope", dt - 2, 3), 0))

              def qmm(e, p=p, dt=dt):
                  last = None
                  for ht in range(HT):
                      for rc in range(4):
                          last = e.matmul(ps_q[(dt % 2) * 4 + rc][:], wq_sb[p][:, ht, :],
                                          lnq[:, ht, rc * 512:(rc + 1) * 512],
                                          start=(ht == 0), stop=(ht == HT - 1),
                                          skip_group_check=True)
                  return last
              tmm = op("tensor", qmm, TE)
              tags[("qmm", dt)] = tmm
              wait("vector", TE, tmm)
              wait("vector", ST, tags.get(("qtst", dt - 2), 0))
              for rc in range(4):
                  rtag = emit_rope(ps_q[(dt % 2) * 4 + rc], cosb, sinb, rc * 512, rope_b,
                                   qt_st[p][:, rc * 512:(rc + 1) * 512].rearrange("p r -> p r"))
                  tags[("rope", dt, rc)] = rtag
              wait("scalar", VE, tags[("rope", dt, 3)])
              tags[("qtst", dt)] = dma("scalar", QT_h[dt, :, :], qt_st[p][:], ST)
          wait("sync", ST, tags[("qtst", QH - 1)])
          barrier()
    _phase_q2()
    if upto < 6:
        return finish(nc, plan, ST)

    def _phase_atn():
      with contextlib.ExitStack() as ph:
          _sb = lambda n, s, d: ph.enter_context(nc.sbuf_tensor(n, s, d))
          _ps = lambda n, s, d=F32: ph.enter_context(nc.psum_tensor(n, s, d))
          qt_sb = _sb("a_qt", [128, QH, 512], BF16)
          kt_b = [_sb(f"a_kt{i}", [128, S], BF16) for i in range(2)]
          v_b = [_sb(f"a_v{i}", [128, 32, 128], BF16) for i in range(2)]
          NPT = 4
          pt = [_sb(f"a_pt{i}", [128, 512], BF16) for i in range(NPT)]
          dm_sb = [_sb("a_dm0", [128, 8, 512], BF16)]
          ot_st = _sb("a_ot", [128, QH, 512], BF16)
          den_sb = [_sb(f"a_dn{i}", [1, 512], F32) for i in range(2)]
          o_stage = [_sb(f"a_os{i}", [128, 512], F32) for i in range(2)]
          rec_sb = [_sb(f"a_rc{i}", [1, 512], F32) for i in range(2)]
          recb = [_sb(f"a_rb{i}", [128, 512], F32) for i in range(2)]
          ps_s = [_ps(f"a_ps{i}", [128, 512]) for i in range(3)]
          ps_o = [_ps(f"a_po{i}", [128, 512]) for i in range(3)]
          ps_d = [_ps(f"a_pd{i}", [128, 512]) for i in range(2)]
          w1q = make_w1_quanta(_sb) if (upto >= 7 and CFG["w1_inline"]) else []
          w1i = [0]

          def drain_w1(n):
              for _ in range(n):
                  if w1i[0] < len(w1q):
                      w1q[w1i[0]]()
                      w1i[0] += 1

          LA = CFG["la"]
          tags = {}
          hi = 0
          gt = 0
          for qc in range(4):
              nst = NST[qc]
              wait("sync", TE, tags.get(("qcdone", qc - 1), 0))
              qt_tag = dma("sync", qt_sb[:],
                           QT_h[:, :, qc * 512:(qc + 1) * 512].rearrange("t p r -> p t r"), LD)
              if qc > 0:
                  wait("sync", VE, tags[("otn", hi - 1)])
              dm_tag = dma("sync", dm_sb[0][:], dmask[:, qc, :, :], LD)
              for kvh in range(KVH):
                  ki = qc * KVH + kvh
                  kp = ki % 2
                  wait("sync", TE, tags.get(("kidone", ki - 2), 0))
                  kl = dma("sync", kt_b[kp][:, 0:nst * 128], KT_h[kvh, :, 0:nst * 128], LD)
                  vl_ = dma("sync", v_b[kp][:, 0:nst, :],
                            V_h[kvh, 0:nst, :, :].rearrange("s p d -> p s d"), LD)
                  wait("tensor", LD, vl_)
                  if kvh == 0:
                      wait("tensor", LD, qt_tag)
                      wait("vector", LD, dm_tag)
                  for qh in range(4):
                      head = kvh * 4 + qh
                      bo = hi % 3
                      bd = hi % 2
                      wait("tensor", VE, tags.get(("otn", hi - 3), 0))
                      wait("tensor", SE, tags.get(("dcp", hi - 2), 0))
                      for st in range(nst + LA):
                          if st < nst:
                              bs = gt % 3
                              wait("tensor", SE, tags.get(("exp", gt - 3), 0))
                              stag = op("tensor", lambda e, bs=bs, st=st, head=head, kp=kp:
                                        e.matmul(ps_s[bs][:], kt_b[kp][:, st * 128:(st + 1) * 128],
                                                 qt_sb[:, head, :], start=True, stop=True,
                                                 skip_group_check=True), TE)
                              wait("scalar", TE, stag)
                              wait("scalar", TE, tags.get(("av", gt - NPT), 0))
                              etag = op("scalar", lambda e, bs=bs, bp=gt % NPT, st=st, qc=qc:
                                        e.activation(pt[bp][:], ps_s[bs][:], AF.Exp,
                                                     bias=bias_sb[:, qc, st:st + 1],
                                                     scale=SCALE), SE)
                              tags[("exp", gt)] = etag
                              if 8 * qc <= st < 8 * qc + 8:
                                  wait("vector", SE, etag)
                                  mtag = op("vector", lambda e, bp=gt % NPT, st=st, qc=qc:
                                            e.tensor_tensor(pt[bp][:], pt[bp][:],
                                                            dm_sb[0][:, st - 8 * qc, :],
                                                            ALU.mult), VE)
                                  tags[("msk", gt)] = mtag
                          if st >= LA:
                              dt_ = st - LA
                              dgt = gt - LA if st < nst else gt - (nst + LA - st)
                              if ("msk", dgt) in tags:
                                  wait("tensor", VE, tags[("msk", dgt)])
                              else:
                                  wait("tensor", SE, tags[("exp", dgt)])
                              op("tensor", lambda e, bd=bd, bp=dgt % NPT, dt_=dt_, nst=nst:
                                 e.matmul(ps_d[bd][0:1, :], ones[:], pt[bp][:],
                                          start=(dt_ == 0), stop=(dt_ == nst - 1),
                                          skip_group_check=True))
                              avt = op("tensor", lambda e, bo=bo, bp=dgt % NPT, dt_=dt_,
                                       nst=nst, kp=kp:
                                       e.matmul(ps_o[bo][:], v_b[kp][:, dt_, :], pt[bp][:],
                                                start=(dt_ == 0), stop=(dt_ == nst - 1),
                                                skip_group_check=True), TE)
                              tags[("av", dgt)] = avt
                          if st < nst:
                              gt += 1
                      def drain_cover(e, bs=(gt + 2) % 3, kp=kp):
                          e.matmul(ps_s[bs][0:1, :], ones[0:1, 0:1], kt_b[kp][0:1, 0:512],
                                   start=False, stop=False, skip_group_check=True)
                          return e.matmul(ps_s[bs][0:1, :], ones[0:1, 0:1],
                                          kt_b[kp][0:1, 0:512], start=False, stop=False,
                                          skip_group_check=True)
                      hdone = op("tensor", drain_cover, TE)
                      wait("scalar", TE, hdone)
                      wait("scalar", VE, tags.get(("rec", hi - 2), 0))
                      dtag = op("scalar", lambda e, bd=bd, hp=hi % 2: e.copy(
                          den_sb[hp][:], ps_d[bd][0:1, :]), SE)
                      tags[("dcp", hi)] = dtag
                      wait("vector", SE, dtag)
                      wait("vector", ST, tags.get(("rst", hi - 2), 0))
                      rtag = op("vector", lambda e, hp=hi % 2: e.reciprocal(
                          rec_sb[hp][:], den_sb[hp][:]), VE)
                      tags[("rec", hi)] = rtag
                      wait("scalar", VE, rtag)
                      rst = dma("scalar", den_h[qc, head:head + 1, :], rec_sb[hi % 2][:], ST)
                      tags[("rst", hi)] = rst
                      wait("sync", ST, rst)
                      wait("sync", VE, tags.get(("otn", hi - 2), 0))
                      rbl = dma("sync", recb[hi % 2][:],
                                den_h[qc, head:head + 1, :].partition_broadcast(128), LD)
                      tags[("rbl", hi)] = rbl
                      wait("scalar", VE, tags.get(("oev-free", hi - 2), 0))
                      oev = op("scalar", lambda e, bo=bo, hp=hi % 2: e.copy(
                          o_stage[hp][:], ps_o[bo][:]), SE)
                      tags[("oev", hi)] = oev
                      wait("vector", SE, oev)
                      wait("vector", LD, rbl)
                      wait("vector", ST, tags.get(("otstore", qc - 1), 0))
                      otn = op("vector", lambda e, head=head, hp=hi % 2: e.tensor_tensor(
                          ot_st[:, head, :], o_stage[hp][:], recb[hp][:], ALU.mult), VE)
                      tags[("otn", hi)] = otn
                      tags[("oev-free", hi)] = otn
                      hi += 1
                      drain_w1(3 if upto >= 7 else 0)
                  tags[("kidone", ki)] = TE.n
              fl = op("vector", lambda e: e.tensor_copy(flush_v[:], flush_v[:]), VE)
              wait("scalar", VE, fl)
              tags[("otstore", qc)] = dma(
                  "scalar", OT_h[:, :, qc * 512:(qc + 1) * 512].rearrange("t p r -> p t r"),
                  ot_st[:], ST)
              tags[("qcdone", qc)] = TE.n
          drain_w1(len(w1q))
          wait("sync", ST, ST.n)
          wait("sync", GP, GP.n)
          wait("sync", LD, LD.n)
          barrier()
    _phase_atn()
    if upto < 7:
        return finish(nc, plan, ST)

    def _phase_oproj():
      with contextlib.ExitStack() as ph:
          _sb = lambda n, s, d: ph.enter_context(nc.sbuf_tensor(n, s, d))
          _ps = lambda n, s, d=F32: ph.enter_context(nc.psum_tensor(n, s, d))
          ot_sb = _sb("o_ot", [128, QH, RO], BF16)
          wo_sb = [_sb(f"o_w{i}", [128, HT, 128], BF16) for i in range(2)]
          yt_sb = [_sb(f"o_yt{i}", [128, 512], BF16) for i in range(2)]
          ps_y = [_ps(f"o_ps{i}", [128, 512]) for i in range(2)]
          lo = dma("sync", ot_sb[:], OT_h[:].rearrange("t p r -> p t r"), LD)
          wait("tensor", LD, lo)
          tags = {}
          g = 0
          for ho in range(HT):
              p = ho % 2
              wait("gpsimd", TE, tags.get(("ymm", ho - 2, 3), 0))
              gw = dma("gpsimd", wo_sb[p][:],
                       woT[:, ho * 128:(ho + 1) * 128].rearrange("(t p) d -> p t d", p=128), GP)
              wait("tensor", GP, gw)
              for rc in range(4):
                  bk = g % 2
                  wait("tensor", SE, tags.get(("yevac", g - 2), 0))

                  def ymm(e, p=p, rc=rc, bk=bk):
                      last = None
                      for dt in range(QH):
                          last = e.matmul(ps_y[bk][:], wo_sb[p][:, dt, :],
                                          ot_sb[:, dt, rc * 512:(rc + 1) * 512],
                                          start=(dt == 0), stop=(dt == QH - 1))
                      return last
                  tmm = op("tensor", ymm, TE)
                  tags[("ymm", ho, rc)] = tmm
                  wait("scalar", TE, tmm)
                  wait("scalar", ST, tags.get(("yst", g - 2), 0))
                  sev = op("scalar", lambda e, bk=bk: e.copy(yt_sb[bk][:], ps_y[bk][:]), SE)
                  tags[("yevac", g)] = sev
                  tags[("yst", g)] = dma("scalar", YT_h[ho, :, rc * 512:(rc + 1) * 512],
                                         yt_sb[bk][:], ST)
                  g += 1
          wait("sync", ST, tags[("yst", g - 1)])
          barrier()
    _phase_oproj()
    if upto < 8:
        return finish(nc, plan, ST)

    def _phase_o2():
      with contextlib.ExitStack() as ph:
          _sb = lambda n, s, d: ph.enter_context(nc.sbuf_tensor(n, s, d))
          _ps = lambda n, s, d=F32: ph.enter_context(nc.psum_tensor(n, s, d))
          ytc = [_sb(f"o2_yt{i}", [128, HT, 512], BF16) for i in range(2)]
          xb = [_sb(f"o2_x{i}", [128, D], F32) for i in range(2)]
          y1b = [_sb(f"o2_y1{i}", [128, D], F32) for i in range(2)]
          sq_scr = _sb("o2_sq", [128, D], F32)
          stat = [_sb(f"o2_st{i}", [128, 8], F32) for i in range(2)]
          lny = [_sb(f"o2_lny{i}", [128, D], BF16) for i in range(2)]
          lst = [_sb(f"o2_ls{i}", [128, HT, 128], BF16) for i in range(2)]
          ps_nat = _ps("o2_nat", [128, D], BF16)
          ps_tr = [_ps(f"o2_ptr{i}", [128, 512], BF16) for i in range(2)]
          tags = {}
          for rc in range(4):
              pc_ = rc % 2
              wait("sync", TE, tags.get(("natdone", rc - 2), 0))
              yl = dma("sync", ytc[pc_][:],
                       YT_h[:, :, rc * 512:(rc + 1) * 512].rearrange("t p r -> p t r"), LD)
              wait("tensor", LD, yl)
              for rsub in range(4):
                  i = rc * 4 + rsub
                  pi = i % 2
                  wait("sync", VE, tags.get(("y1", i - 2), 0))
                  lx = dma("sync", xb[pi][:], x_own[i * 128:(i + 1) * 128, :], LD)
                  wait("tensor", VE, tags.get(("y1", i - 1), 0))

                  def natf(e, pc_=pc_, rsub=rsub):
                      last = None
                      for ho in range(HT):
                          last = e.matmul(ps_nat[:, ho * 128:(ho + 1) * 128],
                                          ytc[pc_][:, ho, rsub * 128:(rsub + 1) * 128], ident[:],
                                          start=(ho % 8 == 0), stop=(ho % 8 == 7),
                                          is_transpose=True, skip_group_check=True)
                      return last
                  nt = op("tensor", natf, TE)
                  tags[("nat", i)] = nt
                  wait("vector", TE, nt)
                  wait("vector", LD, lx)
                  wait("vector", ST, tags.get(("y1st", i - 2), 0))
                  vy = op("vector", lambda e, pi=pi: e.tensor_tensor(
                      y1b[pi][:], ps_nat[:], xb[pi][:], ALU.add), VE)
                  tags[("y1", i)] = vy
                  wait("scalar", VE, vy)
                  tags[("y1st", i)] = dma("scalar", y1_h[i * 128:(i + 1) * 128, :], y1b[pi][:], ST)
                  wait("scalar", LD, lx)
                  vl = emit_ln_tile(y1b[pi], stat[pi], lny[pi], sq_scr, tags, 100 + i, 100 + i - 2)
                  wait("tensor", ST, tags.get(("lnyst", i - 2), 0))
                  emit_transpose_tile(lny[pi], lst[pi], 0, ps_tr, tags, 1000 + i * 8, vl)
                  lastev = tags[("trev", 1000 + i * 8 + 7)]
                  wait("scalar", SE, lastev)
                  tags[("lnyst", i)] = dma(
                      "scalar",
                      lnyT_h[:, :, i * 128:(i + 1) * 128].rearrange("t p r -> p t r"),
                      lst[pi][:], ST)
              tags[("natdone", rc)] = TE.n
          wait("sync", ST, tags[("lnyst", 15)])
          barrier()
    _phase_o2()
    if upto < 9:
        return finish(nc, plan, ST)

    def _phase_gate():
      with contextlib.ExitStack() as ph:
          _sb = lambda n, s, d: ph.enter_context(nc.sbuf_tensor(n, s, d))
          _ps = lambda n, s, d=F32: ph.enter_context(nc.psum_tensor(n, s, d))
          lnyT = _sb("g_lny", [128, HT, RO], BF16)
          wg_sb = [_sb(f"g_wg{i}", [128, HT, 128], BF16) for i in range(2)]
          wu_sb = [_sb(f"g_wu{i}", [128, HT, 128], BF16) for i in range(2)]
          s1_sb = [_sb(f"g_s1{i}", [128, 512], BF16) for i in range(2)]
          h_st = [_sb(f"g_h{i}", [128, RO], BF16) for i in range(2)]
          ps_g = [_ps(f"g_pg{i}", [128, 512]) for i in range(2)]
          ps_u = [_ps(f"g_pu{i}", [128, 512]) for i in range(2)]
          ll = dma("sync", lnyT[:], lnyT_h[:].rearrange("t p r -> p t r"), LD)
          wait("tensor", LD, ll)
          tags = {}
          for t in range(TD):
              p = t % 2
              wait("sync", TE, tags.get(("gdone", t - 2), 0))
              dma("sync", wg_sb[p][:],
                  Wgu_h[:, t * 128:(t + 1) * 128].rearrange("(h p) i -> p h i", p=128), LD)
              lw = dma("sync", wu_sb[p][:],
                  Wgu_h[:, (t + TD) * 128:(t + TD + 1) * 128].rearrange("(h p) i -> p h i", p=128), LD)
              wait("tensor", LD, lw)
              for rc in range(4):
                  g = t * 4 + rc
                  bk = g % 2
                  wait("tensor", VE, tags.get(("hm", g - 2), 0))

                  def gmm(e, p=p, rc=rc, bk=bk):
                      last = None
                      for ht in range(HT):
                          last = e.matmul(ps_g[bk][:], wg_sb[p][:, ht, :],
                                          lnyT[:, ht, rc * 512:(rc + 1) * 512],
                                          start=(ht == 0), stop=(ht == HT - 1))
                      for ht in range(HT):
                          last = e.matmul(ps_u[bk][:], wu_sb[p][:, ht, :],
                                          lnyT[:, ht, rc * 512:(rc + 1) * 512],
                                          start=(ht == 0), stop=(ht == HT - 1),
                                          skip_group_check=True)
                      return last
                  tmm = op("tensor", gmm, TE)
                  wait("scalar", TE, tmm)
                  wait("scalar", VE, tags.get(("hm", g - 2), 0))
                  stag = op("scalar", lambda e, bk=bk: e.activation(
                      s1_sb[bk][:], ps_g[bk][:], AF.Silu), SE)
                  wait("vector", SE, stag)
                  if rc == 0:
                      wait("vector", ST, tags.get(("hst", t - 2), 0))
                  hm = op("vector", lambda e, bk=bk, p=p, rc=rc: e.tensor_tensor(
                      h_st[p][:, rc * 512:(rc + 1) * 512], ps_u[bk][:], s1_sb[bk][:],
                      ALU.mult), VE)
                  tags[("hm", g)] = hm
              wait("scalar", VE, tags[("hm", t * 4 + 3)])
              tags[("hst", t)] = dma("scalar", H_h[t, :, :], h_st[p][:], ST)
              tags[("gdone", t)] = TE.n
          wait("sync", ST, tags[("hst", TD - 1)])
          barrier()
    _phase_gate()
    if upto < 10:
        return finish(nc, plan, ST)

    def _phase_down():
      with contextlib.ExitStack() as ph:
          _sb = lambda n, s, d: ph.enter_context(nc.sbuf_tensor(n, s, d))
          _ps = lambda n, s, d=F32: ph.enter_context(nc.psum_tensor(n, s, d))
          hc = _sb("d_hc", [128, TD, 512], BF16)
          wdh = [_sb(f"d_w{i}", [128, TD, 128], BF16) for i in range(2)]
          dstage = _sb("d_ds", [128, HT, 512], BF16)
          y1b = [_sb("d_y10", [128, D], F32)]
          ob = [_sb("d_o0", [128, D], F32)]
          ps_d = [_ps(f"d_pd{i}", [128, 512]) for i in range(2)]
          ps_nat = _ps("d_nat", [128, D], BF16)
          tags = {}
          for rc in range(4):
              wait("sync", TE, tags.get(("mmdone", rc - 1), 0))
              hl = dma("sync", hc[:], H_h[:, :, rc * 512:(rc + 1) * 512].rearrange("t p r -> p t r"), LD)
              for ho in range(HT):
                  p = ho % 2
                  g = rc * HT + ho
                  wait("sync", TE, tags.get(("dmm", g - 2), 0))
                  lw = dma("sync", wdh[p][:],
                           Wdn_h[:, :, ho * 128:(ho + 1) * 128].rearrange("t p h -> p t h"), LD)
                  wait("tensor", LD, lw)
                  if ho == 0:
                      wait("tensor", LD, hl)
                  wait("tensor", SE, tags.get(("devac", g - 2), 0))
                  if ho == 0:
                      wait("scalar", TE, tags.get(("natlast", rc - 1), 0))

                  def dmm(e, p=p, bk=g % 2):
                      last = None
                      for t in range(TD):
                          last = e.matmul(ps_d[bk][:], wdh[p][:, t, :], hc[:, t, :],
                                          start=(t == 0), stop=(t == TD - 1))
                      return last
                  tmm = op("tensor", dmm, TE)
                  tags[("dmm", g)] = tmm
                  wait("scalar", TE, tmm)
                  sev = op("scalar", lambda e, ho=ho, bk=g % 2: e.copy(
                      dstage[:, ho, :], ps_d[bk][:]), SE)
                  tags[("devac", g)] = sev
              tags[("devac-last", rc)] = tags[("devac", rc * HT + HT - 1)]
              tags[("mmdone", rc)] = TE.n
              wait("tensor", SE, tags[("devac-last", rc)])
              for rsub in range(4):
                  i = rc * 4 + rsub
                  pi = 0
                  wait("sync", VE, tags.get(("y2", i - 1), 0))
                  ly = dma("sync", y1b[pi][:], y1_h[i * 128:(i + 1) * 128, :], LD)
                  wait("tensor", VE, tags.get(("y2", i - 1), 0))

                  def natf(e, rsub=rsub):
                      last = None
                      for ho in range(HT):
                          last = e.matmul(ps_nat[:, ho * 128:(ho + 1) * 128],
                                          dstage[:, ho, rsub * 128:(rsub + 1) * 128], ident[:],
                                          start=(ho % 8 == 0), stop=(ho % 8 == 7),
                                          is_transpose=True, skip_group_check=True)
                      return last
                  nt = op("tensor", natf, TE)
                  if rsub == 3:
                      tags[("natlast", rc)] = nt
                  wait("vector", TE, nt)
                  wait("vector", LD, ly)
                  wait("vector", ST, tags.get(("ost", i - 1), 0))
                  vy = op("vector", lambda e, pi=pi: e.tensor_tensor(
                      ob[pi][:], ps_nat[:], y1b[pi][:], ALU.add), VE)
                  tags[("y2", i)] = vy
                  wait("scalar", VE, vy)
                  tags[("ost", i)] = dma("scalar", out[i * 128:(i + 1) * 128, :], ob[pi][:], ST)
          wait("sync", ST, tags[("ost", 15)])
          barrier()

    _phase_down()
    return finish(nc, plan, ST)


def finish(nc, plan, ST):
    with nc.Block() as block:
        @block.sync
        def _(eng):
            for fn in plan.ops["sync"]:
                fn(eng)

        @block.scalar
        def _(eng):
            for fn in plan.ops["scalar"]:
                fn(eng)

        @block.vector
        def _(eng):
            for fn in plan.ops["vector"]:
                fn(eng)

        @block.tensor
        def _(eng):
            for fn in plan.ops["tensor"]:
                fn(eng)

        @block.gpsimd
        def _(eng):
            for fn in plan.ops["gpsimd"]:
                fn(eng)
    return nc



def _rope_tables():
    inv = 1.0 / (10000.0 ** (np.arange(0, 128, 2, dtype=np.float64) / 128.0))
    pos = np.arange(S, dtype=np.float64)
    fr = pos[:, None] * inv[None, :]
    emb = np.concatenate([fr, fr], axis=1)
    cos = np.cos(emb).astype(np.float32).T.copy()
    sin = np.sin(emb).astype(np.float32).T.copy()
    sin_hat = sin.copy()
    sin_hat[0:64] *= -1.0
    return cos, sin_hat


def _core_inputs(inputs):
    x = np.asarray(inputs["L0_attn_hidden_states"], np.float32)
    wq = np.asarray(inputs["L0_attn_Wq"], np.float32)
    wk = np.asarray(inputs["L0_attn_Wk"], np.float32)
    wv = np.asarray(inputs["L0_attn_Wv"], np.float32)
    wo = np.asarray(inputs["L0_attn_Wo"], np.float32)
    qwg = np.asarray(inputs["L0_ffn_qweight_gu"], np.int32)
    qzg = np.asarray(inputs["L0_ffn_qzeros_gu"], np.int32)
    scg = np.asarray(inputs["L0_ffn_scales_gu"], np.float32)
    qwd = np.asarray(inputs["L0_ffn_qweight_dn"], np.int32)
    qzd = np.asarray(inputs["L0_ffn_qzeros_dn"], np.int32)
    scd = np.asarray(inputs["L0_ffn_scales_dn"], np.float32)

    shared = {
        "wqT": np.ascontiguousarray(wq.T[PH_PERM]),
        "wkT": np.ascontiguousarray(wk.T[PH_PERM]),
        "wvT": np.ascontiguousarray(wv.T[PH_PERM]),
        "woT": np.ascontiguousarray(wo.T[:, PH_PERM]),
        "qw_gu": np.ascontiguousarray(qwg[PH_PERM]),
        "qz_gu": qzg,
        "sc_gu": np.ascontiguousarray(scg[:, PI_PERM]),
        "qw_dn": np.ascontiguousarray(qwd[PI_DN]),
        "qz_dn": qzd,
        "sc_dn": np.ascontiguousarray(scd[:, PH_PERM]),
    }
    cos, sin_hat = _rope_tables()
    shared["cos_k"] = cos
    shared["sin_k"] = sin_hat

    maps = []
    for c in range(8):
        b, half = c // 2, c % 2
        chunks = CHUNKS[half]
        xb = np.ascontiguousarray(x[b][:, PH_PERM])
        rows = np.concatenate([np.arange(ch * 512, (ch + 1) * 512) for ch in chunks])
        m = dict(shared)
        m["x_kv"] = xb
        m["x_own"] = np.ascontiguousarray(xb[rows])
        m["cos_q"] = np.ascontiguousarray(cos[:, rows])
        m["sin_q"] = np.ascontiguousarray(sin_hat[:, rows])
        bias = np.zeros((128, 4, 32), np.float32)
        for j, ch in enumerate(chunks):
            nst_true = (ch + 1) * 4
            bias[:, j, nst_true:] = NEG
        m["biascfg"] = bias
        dm = np.zeros((128, 4, 8, 512), np.float32)
        p = np.arange(128)
        r = np.arange(512)
        for j, ch in enumerate(chunks):
            for site in range(8):
                st = 8 * j + site
                sglob = st * 128 + p
                rglob = ch * 512 + r
                dm[:, j, site, :] = (sglob[:, None] <= rglob[None, :]).astype(np.float32)
        m["dmask"] = dm.astype(ml_dtypes.bfloat16)
        maps.append(m)
    return maps


_prog_cache = {}


def kernel(**inputs):
    maps = _core_inputs(inputs)
    key = (DBG_FLAGS["dbg"], DBG_FLAGS["upto"], CFG["w1_inline"], CFG["la"])
    if key not in _prog_cache:
        _prog_cache[key] = build_program(dbg=key[0], upto=key[1])
    nc = _prog_cache[key]
    res = run_bass_kernel_spmd(nc, maps, core_ids=list(range(8)))
    if DBG_FLAGS["dbg"]:
        kernel.last_results = res.results
    y = np.empty((4, S, D), np.float32)
    for c in range(8):
        b, half = c // 2, c % 2
        o = np.asarray(res.results[c]["out"])
        for j, ch in enumerate(CHUNKS[half]):
            blk = np.empty((512, D), np.float32)
            blk[:, PH_PERM] = o[j * 512:(j + 1) * 512]
            y[b, ch * 512:(ch + 1) * 512] = blk
    return y


DBG_FLAGS = {"dbg": False, "upto": 99}
